# version 01
# baseline (speedup 1.0000x reference)
import math
import os
import numpy as np
import ml_dtypes
import concourse.bass as bass
import concourse.mybir as mybir
from concourse.bass_utils import run_bass_kernel_spmd
from concourse.replica_groups import maybe_share_collective_output_space
from contextlib import ExitStack

F32 = mybir.dt.float32
I32 = mybir.dt.int32
BF16 = mybir.dt.bfloat16
AF = mybir.ActivationFunctionType
ALU = mybir.AluOpType
AX = mybir.AxisListType
bfdt = ml_dtypes.bfloat16

NCORE = 8
S, E, NH, HD, II = 8192, 2048, 16, 128, 8192
T = 1024
NT = 8
RG = [list(range(NCORE))]
CSHIFT = 14.0
KILL = -30000.0
C2048 = float(1.0 / math.sqrt(2048.0))
C8192 = float(1.0 / math.sqrt(8192.0))
WCNT = float(8192 * 2048)

DEBUG_TAPS = False
TAP_LIST = ()
PH = 9
P3S = 9
KOG = 12
KEVAC = 1
KF = int(os.environ.get("KF", "16"))
KNWR = int(os.environ.get("KNWR", "4"))
PH = 9
P3S = 9
KOG = 12
KEVAC = 1
KF = int(os.environ.get("KF", "16"))
KNWR = int(os.environ.get("KNWR", "4"))


def hadamard(n):
    h = np.array([[1.0]], dtype=np.float64)
    while h.shape[0] < n:
        h = np.block([[h, h], [h, -h]])
    return h


class Eng:
    def __init__(self, e, sem, dma_sem=None):
        self.e = e
        self.sem = sem
        self.dsem = dma_sem
        self.n = 0
        self.dn = 0

    def dma(self, out, in_, **kw):
        self.e.dma_start(out=out, in_=in_, **kw).then_inc(self.dsem, 16)
        self.dn += 16
        return self.dn

    def dmaT(self, out, in_):
        self.e.dma_start_transpose(out, in_).then_inc(self.dsem, 16)
        self.dn += 16
        return self.dn

    def inc(self, instr):
        instr.then_inc(self.sem, 1)
        self.n += 1
        return self.n

    def wait(self, other, val=None):
        v = other.n if val is None else val
        if v > 0:
            self.e.wait_ge(other.sem, v)

    def wait_dma(self, other, val=None):
        v = other.dn if val is None else val
        if v > 0:
            self.e.wait_ge(other.dsem, v)


def make_inmaps(L0_attn_x, L0_attn_Wqkv, L0_attn_Wo, L0_ffn_w_gate, L0_ffn_w_up, L0_ffn_w_down):
    x = np.ascontiguousarray(np.asarray(L0_attn_x, dtype=np.float32).reshape(S, E))
    wqkv = np.ascontiguousarray(np.asarray(L0_attn_Wqkv, dtype=np.float32))
    wo = np.ascontiguousarray(np.asarray(L0_attn_Wo, dtype=np.float32))
    wg = np.asarray(L0_ffn_w_gate, dtype=np.float32)
    wu = np.asarray(L0_ffn_w_up, dtype=np.float32)
    wd = np.asarray(L0_ffn_w_down, dtype=np.float32)

    h128 = hadamard(128).astype(bfdt)
    i = np.arange(HD // 2, dtype=np.float32)
    inv_freq = (1.0 / 10000.0 ** (2.0 * i / HD)).astype(np.float32)
    pos = np.arange(S, dtype=np.float32)
    th = np.outer(pos, inv_freq).astype(np.float32)
    cosf = np.cos(th).astype(np.float32)
    sinf = np.sin(th).astype(np.float32)
    qs = np.float32(1.0 / math.sqrt(HD))
    kk = np.arange(512)
    dmask = np.where(kk[:, None] <= kk[None, :], 0.0, KILL).astype(bfdt)

    in_maps = []
    for c in range(NCORE):
        rows_b = slice(512 * c, 512 * c + 512)
        rows_a = slice(8192 - 512 * (c + 1), 8192 - 512 * c)
        own = np.concatenate([x[rows_b], x[rows_a]], axis=0)
        cos_own = np.concatenate([cosf[rows_b], cosf[rows_a]], axis=0)
        sin_own = np.concatenate([sinf[rows_b], sinf[rows_a]], axis=0)
        kb = np.full((32,), -CSHIFT, np.float32)
        for t_ in range(1, 16):
            if t_ - 1 >= 15 - c:
                kb[t_] = KILL
        for t_ in range(1, 8):
            if t_ - 1 >= c:
                kb[16 + t_] = KILL
        kb[24:] = KILL
        in_maps.append({
            "x_own": own,
            "wqkv": wqkv,
            "wo": wo,
            "wg_sh": np.ascontiguousarray(wg[1024 * c:1024 * (c + 1)]),
            "wu_sh": np.ascontiguousarray(wu[1024 * c:1024 * (c + 1)]),
            "wd_sh": np.ascontiguousarray(wd[:, 1024 * c:1024 * (c + 1)]),
            "h128": h128,
            "cosq": (cos_own * qs).astype(np.float32),
            "sinq": (sin_own * qs).astype(np.float32),
            "cosk": cos_own,
            "sink": sin_own,
            "killb": np.broadcast_to(kb, (128, 32)).copy(),
            "dmask": dmask,
        })

    return in_maps


def kernel(**inputs):
    in_maps = make_inmaps(**inputs)
    nc = build_nc()
    ncr = NCORE
    res = run_bass_kernel_spmd(nc, in_maps, core_ids=list(range(NCORE)))

    out = np.zeros((S, E), np.float32)
    for c in range(ncr):
        r = res.results[c]["out_own"]
        out[512 * c:512 * c + 512] = r[0:512]
        out[8192 - 512 * (c + 1):8192 - 512 * c] = r[512:1024]
    kernel.taps = res.results
    return out.reshape(1, S, E)


def build_nc():
    nc = bass.Bass("TRN2", target_bir_lowering=False)

    x_in = nc.dram_tensor("x_own", [T, E], F32, kind="ExternalInput").ap()
    wqkv_in = nc.dram_tensor("wqkv", [E, 3 * E], F32, kind="ExternalInput").ap()
    wo_in = nc.dram_tensor("wo", [E, E], F32, kind="ExternalInput").ap()
    wg_in = nc.dram_tensor("wg_sh", [1024, E], F32, kind="ExternalInput").ap()
    wu_in = nc.dram_tensor("wu_sh", [1024, E], F32, kind="ExternalInput").ap()
    wd_in = nc.dram_tensor("wd_sh", [E, 1024], F32, kind="ExternalInput").ap()
    h128_in = nc.dram_tensor("h128", [128, 128], BF16, kind="ExternalInput").ap()
    cosq_in = nc.dram_tensor("cosq", [T, 64], F32, kind="ExternalInput").ap()
    sinq_in = nc.dram_tensor("sinq", [T, 64], F32, kind="ExternalInput").ap()
    cosk_in = nc.dram_tensor("cosk", [T, 64], F32, kind="ExternalInput").ap()
    sink_in = nc.dram_tensor("sink", [T, 64], F32, kind="ExternalInput").ap()
    killb_in = nc.dram_tensor("killb", [128, 32], F32, kind="ExternalInput").ap()
    dmask_in = nc.dram_tensor("dmask", [512, 512], BF16, kind="ExternalInput").ap()
    out_d = nc.dram_tensor("out_own", [T, E], F32, kind="ExternalOutput").ap()
    tp_xchS = nc.dram_tensor("tp_xchS", [T, E], BF16, kind="ExternalOutput").ap() if DEBUG_TAPS else None

    taps = {}

    def tap(name, shape, dtype=F32):
        if DEBUG_TAPS and name in TAP_LIST:
            taps[name] = nc.dram_tensor(name, shape, dtype, kind="ExternalOutput").ap()
            return taps[name]
        return None

    xchi_d = nc.dram_tensor("xchi_d", [T, E], BF16).ap()
    xclo_d = nc.dram_tensor("xclo_d", [T, E], BF16).ap()
    q_tm = nc.dram_tensor("q_tm", [T, E], BF16).ap()
    k_tm = nc.dram_tensor("k_tm", [T, E], BF16).ap()
    kT_own = nc.dram_tensor("kT_own", [E, T], BF16).ap()
    v_own_d = nc.dram_tensor("v_own", [T, E], BF16).ap()
    sp_ag = maybe_share_collective_output_space("AllGather", RG)

    kT_all = nc.dram_tensor("kT_all", [NCORE, E, T], BF16, addr_space=sp_ag).ap()
    v_all = nc.dram_tensor("v_all", [NCORE, T, E], BF16, addr_space=sp_ag).ap()
    wg_tern = nc.dram_tensor("wg_tern", [1024, E], BF16).ap()
    wu_tern = nc.dram_tensor("wu_tern", [1024, E], BF16).ap()
    wd_tern = nc.dram_tensor("wd_tern", [E, 1024], BF16).ap()
    wgT_own = nc.dram_tensor("wgT_own", [E, 1024], BF16).ap()
    wuT_own = nc.dram_tensor("wuT_own", [E, 1024], BF16).ap()
    wdT_own = nc.dram_tensor("wdT_own", [1024, E], BF16).ap()
    wgT_all = nc.dram_tensor("wgT_all", [NCORE, E, 1024], BF16, addr_space=sp_ag).ap()
    wuT_all = nc.dram_tensor("wuT_all", [NCORE, E, 1024], BF16, addr_space=sp_ag).ap()
    wdT_all = nc.dram_tensor("wdT_all", [NCORE, 1024, E], BF16, addr_space=sp_ag).ap()
    sp_ar = maybe_share_collective_output_space("AllReduce", RG)

    ws_in = nc.dram_tensor("ws_in", [1, 8], F32).ap()
    ws_out = nc.dram_tensor("ws_out", [1, 8], F32, addr_space=sp_ar).ap()
    part_d = nc.dram_tensor("part_d", [4, 128], F32).ap()
    scal_d = nc.dram_tensor("scal_d", [1, 64], F32).ap()
    y1hi_d = nc.dram_tensor("y1hi_d", [E, T], BF16).ap()
    y1lo_d = nc.dram_tensor("y1lo_d", [E, T], BF16).ap()
    q1_tm = nc.dram_tensor("q1_tm", [T, E], BF16).ap()
    a_row_d = nc.dram_tensor("a_row_d", [1, T], F32).ap()
    y2hi_d = nc.dram_tensor("y2hi_d", [II, 256], BF16).ap()
    y2lo_d = nc.dram_tensor("y2lo_d", [II, 256], BF16).ap()
    q2_tm = nc.dram_tensor("q2_tm", [256, II], BF16).ap()
    acc_d = nc.dram_tensor("acc_d", [T, E], F32).ap()
    linv_d = nc.dram_tensor("linv_d", [1, NH * 512], F32).ap()

    with ExitStack() as _st:
        sb = lambda *a: _st.enter_context(nc.sbuf_tensor(*a))
        sem = lambda n: _st.enter_context(nc.semaphore(n))
        ones_b = sb("ones_b", [128, 1], BF16)
        h128t = sb("h128t", [128, 128], BF16)
        killbt = sb("killbt", [128, 32], F32)
        dmt = sb("dmt", [128, 2048], BF16)
        cosqt = sb("cosqt", [128, 512], F32)
        sinqt = sb("sinqt", [128, 512], F32)
        coskt = sb("coskt", [128, 512], F32)
        sinkt = sb("sinkt", [128, 512], F32)
        rstd = sb("rstd", [128, 8], F32)
        a1c = sb("a1c", [128, 8], F32)
        dsc = sb("dsc", [128, 8], F32)
        wsc_bc = sb("wsc_bc", [128, 8], F32)
        tiny = sb("tiny", [128, 16], F32)
        sQd = sem("sQ"); gQd = sem("gQ"); aQd = sem("aQ"); vQd = sem("vQ")
        tS = sem("tS"); vS = sem("vS"); sS = sem("sS"); gS = sem("gS"); ccS = sem("ccS")
        wrS = [sem(f"wrS{i}") for i in range(4)]
        guS = [sem(f"guS{i}") for i in range(2)]
        wdS = [sem(f"wdS{i}") for i in range(2)]
        woS = [sem(f"woS{i}") for i in range(2)]
        kvS = [sem(f"kvS{i}") for i in range(2)]
        slotcnt = {}

        def slot_dma(eng, semh, out, in_, **kw):
            eng.e.dma_start(out=out, in_=in_, **kw).then_inc(semh, 16)
            slotcnt[id(semh)] = slotcnt.get(id(semh), 0) + 16
            return slotcnt[id(semh)]
        block = _st.enter_context(nc.Block())

        sy = Eng(nc.sync, None, sQd)
        gp = Eng(nc.gpsimd, gS, gQd)
        sc = Eng(nc.scalar, sS, aQd)
        ve = Eng(nc.vector, vS, vQd)
        te = Eng(nc.tensor, tS, None)
        cc_n = [0]
        ALL = [sy, gp, sc, ve, te]

        def cc(kind, op, ins, outs):
            gp.e.collective_compute(kind, op, replica_groups=RG, ins=ins, outs=outs).then_inc(ccS, 1)
            cc_n[0] += 1
            return cc_n[0]

        def barrier(ccv=None):
            snap_d = [(o, o.dn) for o in (sy, gp, sc, ve)]
            snap_n = [(o, o.n) for o in (gp, sc, ve, te)]
            ccv = cc_n[0] if ccv is None else ccv
            for sect, w in ((block.sync, sy), (block.gpsimd, gp), (block.scalar, sc),
                            (block.vector, ve), (block.tensor, te)):
                def _f(e, w=w):
                    for o, v in snap_d:
                        if v > 0:
                            w.e.wait_ge(o.dsem, v)
                    for o, v in snap_n:
                        if v > 0:
                            w.e.wait_ge(o.sem, v)
                    if ccv > 0:
                        w.e.wait_ge(ccS, ccv)
                sect(_f)

        @block.sync
        def _(e):
            sy.dma(h128t[:], h128_in[:])
            sy.dma(killbt[:], killb_in[:])
            sy.dma(dmt[:], bass.AP(dmask_in.tensor, 0, [[512, 128], [65536, 4], [1, 512]]))
            for j in range(NT):
                sy.dma(cosqt[:, 64 * j:64 * (j + 1)], cosq_in[128 * j:128 * (j + 1), :])
                sy.dma(sinqt[:, 64 * j:64 * (j + 1)], sinq_in[128 * j:128 * (j + 1), :])
                sy.dma(coskt[:, 64 * j:64 * (j + 1)], cosk_in[128 * j:128 * (j + 1), :])
                sy.dma(sinkt[:, 64 * j:64 * (j + 1)], sink_in[128 * j:128 * (j + 1), :])

        @block.vector
        def _(e):
            ve.inc(ve.e.memset(ones_b[:], 1.0))

        with ExitStack() as _s1:
            sb1 = lambda *a: _s1.enter_context(nc.sbuf_tensor(*a))
            xt = sb1("xt", [128, E], F32)
            sqs = sb1("sqs", [128, E], F32)
            xc = sb1("xc", [128, E], F32)
            xch = sb1("xch", [128, E], BF16)
            xcl = sb1("xcl", [128, E], BF16)

            for j in range(NT):
                ld = [0]

                @block.sync
                def _(e, j=j, ld=ld):
                    sy.wait(sc)
                    sy.wait(ve)
                    ld[0] = sy.dma(xt[:], x_in[128 * j:128 * (j + 1), :])

                @block.scalar
                def _(e, j=j, ld=ld):
                    sc.wait_dma(sy, ld[0])
                    sc.inc(sc.e.activation(sqs[:], xt[:], AF.Square, bias=0.0, scale=1.0,
                                           accum_out=tiny[:, 0:1]))

                @block.vector
                def _(e, j=j, ld=ld):
                    ve.wait_dma(sy, ld[0])
                    ve.e.tensor_reduce(tiny[:, 5:6], xt[:], AX.X, ALU.add)
                    ve.e.drain()
                    ve.e.tensor_scalar(tiny[:, 1:2], tiny[:, 5:6], 1.0 / E, None, ALU.mult)
                    ve.wait(sc)
                    ve.e.tensor_scalar(tiny[:, 6:7], tiny[:, 0:1], 1.0 / E, None, ALU.mult)
                    ve.e.drain()
                    ve.e.tensor_tensor(tiny[:, 3:4], tiny[:, 1:2], tiny[:, 1:2], ALU.mult)
                    ve.e.drain()
                    ve.e.tensor_tensor(tiny[:, 7:8], tiny[:, 6:7], tiny[:, 3:4], ALU.subtract)
                    ve.e.drain()
                    ve.e.tensor_scalar(tiny[:, 2:3], tiny[:, 7:8], 1e-5, None, ALU.add)
                    ve.e.drain()
                    ve.inc(ve.e.reciprocal(tiny[:, 4:5], tiny[:, 2:3]))
                    ve.e.drain()

                @block.scalar
                def _(e, j=j):
                    sc.wait(ve)
                    sc.inc(sc.e.activation(tiny[:, 8:9], tiny[:, 4:5], AF.Sqrt, bias=0.0, scale=1.0))

                @block.vector
                def _(e, j=j):
                    ve.wait(sc)
                    for _it in range(2):
                        ve.e.tensor_tensor(tiny[:, 9:10], tiny[:, 8:9], tiny[:, 8:9], ALU.mult)
                        ve.e.drain()
                        ve.e.tensor_tensor(tiny[:, 10:11], tiny[:, 9:10], tiny[:, 2:3], ALU.mult)
                        ve.e.drain()
                        ve.e.tensor_scalar(tiny[:, 11:12], tiny[:, 10:11], -0.5, 1.5, ALU.mult, ALU.add)
                        ve.e.drain()
                        ve.e.tensor_tensor(tiny[:, 8:9], tiny[:, 8:9], tiny[:, 11:12], ALU.mult)
                        ve.e.drain()
                    ve.e.tensor_copy(rstd[:, j:j + 1], tiny[:, 8:9])
                    ve.e.drain()
                    ve.wait_dma(gp)
                    ve.e.tensor_scalar(xc[:], xt[:], tiny[:, 1:2], None, ALU.subtract)
                    ve.e.drain()
                    ve.e.tensor_copy(xch[:], xc[:])
                    ve.e.drain()
                    ve.inc(ve.e.tensor_tensor(xcl[:], xc[:], xch[:], ALU.subtract))

                @block.gpsimd
                def _(e, j=j):
                    gp.wait(ve)
                    gp.dma(xchi_d[128 * j:128 * (j + 1), :], xch[:])
                    gp.dma(xclo_d[128 * j:128 * (j + 1), :], xcl[:])
                    if DEBUG_TAPS:
                        gp.dma(tp_xchS[128 * j:128 * (j + 1), :], xch[:])


            barrier()

        _sxc = ExitStack()
        XCHI_T = _sxc.enter_context(nc.sbuf_tensor("XCHI_T", [128, 16384], BF16))
        XCLO_T = _sxc.enter_context(nc.sbuf_tensor("XCLO_T", [128, 16384], BF16))

        @block.scalar
        def _(e):
            for a in range(16):
                sc.dmaT(XCHI_T[:, 1024 * a:1024 * (a + 1)], xchi_d[:, 128 * a:128 * (a + 1)])
                sc.dmaT(XCLO_T[:, 1024 * a:1024 * (a + 1)], xclo_d[:, 128 * a:128 * (a + 1)])

        if PH >= 2:
            with ExitStack() as _s2:
                sb2 = lambda *a: _s2.enter_context(nc.sbuf_tensor(*a))
                wt = sb2("wt", [128, E], F32)
                wtern = sb2("wtern", [128, E], BF16)
                wqi = sb2("wqi", [128, E], I32)
                prow = sb2("prow", [1, 128], F32)
                srow = sb2("srow", [1, 8], F32)
                wtt = sb2("wtt", [128, E], BF16)

                @block.vector
                def _(e):
                    ve.inc(ve.e.memset(srow[:], 0.0))

                mats = [(wg_in, 8, E, wg_tern), (wu_in, 8, E, wu_tern), (wd_in, 16, 1024, wd_tern)]
                for mi, (w_in, ntile, tw, _w) in enumerate(mats):
                    for t_ in range(ntile):
                        ld = [0]

                        @block.sync
                        def _(e, mi=mi, t_=t_, w_in=w_in, tw=tw, ld=ld):
                            sy.wait(ve)
                            ld[0] = sy.dma(wt[:, 0:tw], w_in[128 * t_:128 * (t_ + 1), :])

                        @block.vector
                        def _(e, mi=mi, t_=t_, tw=tw, ld=ld, first=(t_ == 0)):
                            ve.wait_dma(sy, ld[0])
                            ve.e.tensor_reduce(tiny[:, 0:1], wt[:, 0:tw], AX.X, ALU.add,
                                               apply_absolute_value=True)
                            ve.e.drain()
                            if first:
                                ve.inc(ve.e.tensor_copy(tiny[:, 8 + mi:9 + mi], tiny[:, 0:1]))
                            else:
                                ve.inc(ve.e.tensor_tensor(tiny[:, 8 + mi:9 + mi],
                                                          tiny[:, 8 + mi:9 + mi], tiny[:, 0:1], ALU.add))
                            ve.e.drain()

                @block.gpsimd
                def _(e):
                    gp.wait(ve)
                    for mi in range(3):
                        gp.dma(bass.AP(part_d.tensor, 128 * mi, [[1, 128], [1, 1]]), tiny[:, 8 + mi:9 + mi])

                for mi in range(3):
                    ld = [0]

                    @block.sync
                    def _(e, mi=mi, ld=ld):
                        sy.wait(ve)
                        sy.wait_dma(gp)
                        ld[0] = sy.dma(prow[:, :], bass.AP(part_d.tensor, 128 * mi, [[0, 1], [1, 128]]))

                    @block.vector
                    def _(e, mi=mi, ld=ld):
                        ve.wait_dma(sy, ld[0])
                        ve.inc(ve.e.tensor_reduce(srow[:, mi:mi + 1], prow[:], AX.X, ALU.add))
                        ve.e.drain()

                @block.gpsimd
                def _(e):
                    gp.wait(ve)
                    gp.dma(ws_in[:], srow[:])
                    gp.e.wait_ge(gQd, gp.dn)
                    cc("AllReduce", ALU.add, [ws_in[:]], [ws_out[:]])

                @block.sync
                def _(e):
                    sy.e.wait_ge(ccS, cc_n[0])
                    sy.dma(srow[:], ws_out[:])

                @block.vector
                def _(e):
                    ve.wait_dma(sy)
                    ve.e.tensor_scalar(srow[:, 4:8], srow[:, 0:4], 1.0 / WCNT, None, ALU.mult)
                    ve.e.drain()
                    ve.e.tensor_scalar(srow[:, 4:8], srow[:, 4:8], 1e-5, None, ALU.max)
                    ve.e.drain()
                    ve.inc(ve.e.reciprocal(srow[:, 0:4], srow[:, 4:8]))
                    ve.e.drain()

                @block.gpsimd
                def _(e):
                    gp.wait(ve)
                    gp.dma(bass.AP(scal_d.tensor, 0, [[0, 1], [1, 8]]), srow[:, 0:8])

                @block.sync
                def _(e):
                    sy.wait_dma(gp)
                    sy.dma(wsc_bc[:, 0:8], bass.AP(scal_d.tensor, 0, [[0, 128], [1, 8]]))

                for mi, (w_in, ntile, tw, wtern_d) in enumerate(mats):
                    for t_ in range(ntile):
                        ld = [0]

                        @block.sync
                        def _(e, mi=mi, t_=t_, w_in=w_in, tw=tw, ld=ld):
                            sy.wait(ve)
                            ld[0] = sy.dma(wt[:, 0:tw], w_in[128 * t_:128 * (t_ + 1), :])

                        @block.vector
                        def _(e, mi=mi, t_=t_, tw=tw, ld=ld):
                            ve.wait_dma(sy)
                            ve.wait_dma(gp)
                            ve.e.tensor_scalar(wt[:, 0:tw], wt[:, 0:tw], wsc_bc[:, mi:mi + 1], None, ALU.mult)
                            ve.e.drain()
                            ve.e.tensor_copy(wqi[:, 0:tw], wt[:, 0:tw])
                            ve.e.drain()
                            ve.inc(ve.e.tensor_scalar(wtern[:, 0:tw], wqi[:, 0:tw], 1.0, -1.0, ALU.min, ALU.max))

                        @block.gpsimd
                        def _(e, mi=mi, t_=t_, tw=tw, wtern_d=wtern_d):
                            gp.wait(ve)
                            gp.dma(wtern_d[128 * t_:128 * (t_ + 1), 0:tw], wtern[:, 0:tw])

                def transpose_out(src_d, dst_d, ncolblk, srows):
                    for a in range(ncolblk):
                        ldc = [0]

                        @block.scalar
                        def _(e, a=a, ldc=ldc):
                            sc.wait_dma(gp)
                            ldc[0] = sc.dmaT(wtt[:, 0:srows], src_d[:, 128 * a:128 * (a + 1)])

                        @block.gpsimd
                        def _(e, a=a, ldc=ldc):
                            gp.wait_dma(sc, ldc[0])
                            gp.dma(dst_d[128 * a:128 * (a + 1), 0:srows], wtt[:, 0:srows])

                transpose_out(wg_tern, wgT_own, 16, 1024)
                transpose_out(wu_tern, wuT_own, 16, 1024)
                transpose_out(wd_tern, wdT_own, 8, 2048)

                @block.gpsimd
                def _(e):
                    gp.e.wait_ge(gQd, gp.dn)
                    cc("AllGather", ALU.bypass, [wgT_own[:]], [wgT_all[:]])
                    cc("AllGather", ALU.bypass, [wuT_own[:]], [wuT_all[:]])
                    cc("AllGather", ALU.bypass, [wdT_own[:]], [wdT_all[:]])

                barrier(ccv=0)
        AGW_CC = cc_n[0]

        if PH >= 3:
            with ExitStack() as _s3:
                sb3 = lambda *a: _s3.enter_context(nc.sbuf_tensor(*a))
                NWR = 4
                wring = [sb3(f"wr{i}", [128, 512], BF16) for i in range(NWR)]
                crq = sb3("crq", [128, 2048], F32)
                srq = sb3("srq", [128, 2048], F32)
                crk = sb3("crk", [128, 2048], F32)
                srk = sb3("srk", [128, 2048], F32)
                rs1 = sb3("rs1", [128, 256], F32)
                rs2 = sb3("rs2", [128, 256], F32)
                rs3 = sb3("rs3", [128, 256], F32)
                rs4 = sb3("rs4", [128, 256], F32)
                ro = sb3("ro", [128, 4096], F32)
                ktt2 = sb3("ktt2", [128, 1024], BF16)
                ps8 = [_s3.enter_context(nc.psum_tensor(f"pq{i}", [128, 512], F32)) for i in range(8)]

                @block.vector
                def _(e):
                    for j in range(NT):
                        for r in range(4):
                            cs_ = slice(256 * j + 64 * r, 256 * j + 64 * (r + 1))
                            js = slice(64 * j, 64 * (j + 1))
                            ve.e.tensor_scalar(crq[:, cs_], cosqt[:, js], rstd[:, j:j + 1], None, ALU.mult)
                            ve.e.tensor_scalar(srq[:, cs_], sinqt[:, js], rstd[:, j:j + 1], None, ALU.mult)
                            ve.e.tensor_scalar(crk[:, cs_], coskt[:, js], rstd[:, j:j + 1], None, ALU.mult)
                            ve.inc(ve.e.tensor_scalar(srk[:, cs_], sinkt[:, js], rstd[:, j:j + 1], None, ALU.mult))
                    ve.e.drain()

                for og in range((KOG if P3S >= 2 else 0)):
                    for grp in range(2):
                        jlist = [4 * grp + jj for jj in range(4)]

                        te_base = te.n
                        gp_base = gp.dn

                        wr_tgt = []

                        @block.gpsimd
                        def _(e, og=og, grp=grp, te_base=te_base, wr_tgt=wr_tgt):
                            for f in range(16):
                                need = te_base - (NWR - 1) + f
                                if need > 0:
                                    gp.e.wait_ge(te.sem, need)
                                wr_tgt.append(slot_dma(gp, wrS[f % NWR], wring[f % NWR][:],
                                                       wqkv_in[128 * f:128 * (f + 1), 512 * og:512 * (og + 1)]))

                        @block.tensor
                        def _(e, og=og, grp=grp, jlist=jlist, gp_base=gp_base, wr_tgt=wr_tgt):
                            te.wait(ve)
                            te.wait(sc)
                            te.wait_dma(sc)
                            for f in range(16):
                                te.e.wait_ge(wrS[f % NWR], wr_tgt[f])
                                for j in jlist:
                                    mm = te.e.matmul(ps8[j][:],
                                                     XCHI_T[:, 1024 * f + 128 * j:1024 * f + 128 * (j + 1)],
                                                     wring[f % NWR][:], start=(f == 0), stop=(f == 15))
                                    if j == jlist[-1]:
                                        te.inc(mm)

                        if KEVAC == 0:
                            @block.vector
                            def _(e):
                                ve.wait(te)
                                for j in jlist:
                                    cp = ve.e.tensor_copy(ro[:, 512 * j:512 * (j + 1)], ps8[j][:])
                                    if j == jlist[-1]:
                                        ve.inc(cp)
                                ve.e.drain()
                        elif og < 8:
                            ctab, stab = (crq, srq) if og < 4 else (crk, srk)
                            dst = q_tm if og < 4 else k_tm
                            col = 512 * (og % 4)

                            @block.vector
                            def _(e, og=og, jlist=jlist, ctab=ctab, stab=stab):
                                ve.wait(te)
                                ve.wait_dma(gp)
                                for j in jlist:
                                    ps = ps8[j]
                                    cj = slice(256 * j, 256 * (j + 1))
                                    ev = bass.AP(ps, 0, [[512, 128], [2, 256]])
                                    od = bass.AP(ps, 1, [[512, 128], [2, 256]])
                                    ve.e.tensor_tensor(rs1[:], ev, ctab[:, cj], ALU.mult)
                                    ve.e.tensor_tensor(rs2[:], od, stab[:, cj], ALU.mult)
                                    ve.e.tensor_tensor(rs3[:], ev, stab[:, cj], ALU.mult)
                                    ve.e.tensor_tensor(rs4[:], od, ctab[:, cj], ALU.mult)
                                    ve.e.drain()
                                    roe = bass.AP(ro, 512 * j, [[4096, 128], [2, 256]])
                                    roo = bass.AP(ro, 512 * j + 1, [[4096, 128], [2, 256]])
                                    ve.e.tensor_tensor(roe, rs1[:], rs2[:], ALU.subtract)
                                    ve.inc(ve.e.tensor_tensor(roo, rs3[:], rs4[:], ALU.add))
                                    ve.e.drain()

                            @block.gpsimd
                            def _(e, jlist=jlist, dst=dst, col=col):
                                gp.wait(ve)
                                for j in jlist:
                                    gp.dma(dst[128 * j:128 * (j + 1), col:col + 512],
                                           ro[:, 512 * j:512 * (j + 1)])
                        else:
                            col = 512 * (og - 8)

                            @block.vector
                            def _(e, jlist=jlist):
                                ve.wait(te)
                                ve.wait_dma(gp)
                                for j in jlist:
                                    op = ve.e.tensor_scalar(ro[:, 512 * j:512 * (j + 1)], ps8[j][:],
                                                            rstd[:, j:j + 1], None, ALU.mult)
                                    if j == jlist[-1]:
                                        ve.inc(op)
                                ve.e.drain()

                            @block.gpsimd
                            def _(e, jlist=jlist, col=col):
                                gp.wait(ve)
                                for j in jlist:
                                    gp.dma(v_own_d[128 * j:128 * (j + 1), col:col + 512],
                                           ro[:, 512 * j:512 * (j + 1)])

                for a in range(16 if P3S >= 3 else 0):
                    ldc = [0]

                    @block.scalar
                    def _(e, a=a, ldc=ldc):
                        sc.wait_dma(gp)
                        ldc[0] = sc.dmaT(ktt2[:], k_tm[:, 128 * a:128 * (a + 1)])

                    @block.gpsimd
                    def _(e, a=a, ldc=ldc):
                        gp.wait_dma(sc, ldc[0])
                        gp.dma(kT_own[128 * a:128 * (a + 1), :], ktt2[:])

                @block.gpsimd
                def _(e):
                    gp.e.wait_ge(gQd, gp.dn)
                    if P3S >= 4:
                        cc("AllGather", ALU.bypass, [kT_own[:]], [kT_all[:]])
                        cc("AllGather", ALU.bypass, [v_own_d[:]], [v_all[:]])

                barrier(ccv=AGW_CC)

        if PH >= 4:
            with ExitStack() as _s4:
                sb4 = lambda *a: _s4.enter_context(nc.sbuf_tensor(*a))
                F1 = sb4("F1", [128, 16384], F32)
                fh = sb4("fh", [128, 1024], BF16)
                fl = sb4("fl", [128, 1024], BF16)
                th_t = sb4("th_t", [128, 2048], BF16)
                tl_t = sb4("tl_t", [128, 2048], BF16)
                fscr = sb4("fscr", [128, 2048], F32)
                mscr = sb4("mscr", [128, 2048], F32)
                q1i = sb4("q1i", [128, 2048], I32)
                q1b = sb4("q1b", [128, 2048], BF16)
                pf = [_s4.enter_context(nc.psum_tensor(f"pf{i}", [128, 512], F32)) for i in range(2)]

                for a in range(16):
                    for half in range(2):
                        colsT = slice(1024 * a + 512 * half, 1024 * a + 512 * (half + 1))
                        idx = (a * 2 + half) % 2

                        @block.tensor
                        def _(e, colsT=colsT, idx=idx, first=(a == 0 and half == 0)):
                            te.wait(sc)
                            if first:
                                te.wait_dma(sc)
                            te.e.matmul(pf[idx][:], h128t[:], XCHI_T[:, colsT], start=True, stop=False)
                            te.inc(te.e.matmul(pf[idx][:], h128t[:], XCLO_T[:, colsT], start=False, stop=True))

                        @block.scalar
                        def _(e, colsT=colsT, idx=idx):
                            sc.wait(te)
                            sc.inc(sc.e.activation(F1[:, colsT], pf[idx][:], AF.Copy, bias=0.0, scale=1.0))

                @block.vector
                def _(e):
                    ve.wait(sc)
                    for st_ in range(4):
                        bs = 1 << st_
                        pairs = [(a, a + bs) for a in range(16) if (a // bs) % 2 == 0]
                        for (pa, pb) in pairs:
                            ve.e.tensor_tensor(F1[:, 1024 * pa:1024 * (pa + 1)],
                                               F1[:, 1024 * pa:1024 * (pa + 1)],
                                               F1[:, 1024 * pb:1024 * (pb + 1)], ALU.add)
                        ve.e.drain()
                        for pi, (pa, pb) in enumerate(pairs):
                            op = ve.e.scalar_tensor_tensor(F1[:, 1024 * pb:1024 * (pb + 1)],
                                                           F1[:, 1024 * pb:1024 * (pb + 1)], -2.0,
                                                           F1[:, 1024 * pa:1024 * (pa + 1)], ALU.mult, ALU.add)
                            if st_ == 3 and pi == len(pairs) - 1:
                                ve.inc(op)
                        ve.e.drain()

                for a in range(16):
                    @block.vector
                    def _(e, a=a):
                        ve.wait_dma(gp)
                        cols = slice(1024 * a, 1024 * (a + 1))
                        ve.e.tensor_copy(fh[:], F1[:, cols])
                        ve.e.drain()
                        ve.inc(ve.e.tensor_tensor(fl[:], F1[:, cols], fh[:], ALU.subtract))

                    @block.gpsimd
                    def _(e, a=a):
                        gp.wait(ve)
                        gp.dma(y1hi_d[128 * a:128 * (a + 1), :], fh[:])
                        gp.dma(y1lo_d[128 * a:128 * (a + 1), :], fl[:])

                for j in range(NT):
                    ldc = [0]

                    @block.scalar
                    def _(e, j=j, ldc=ldc):
                        sc.wait_dma(gp)
                        sc.wait(ve)
                        sc.dmaT(th_t[:], y1hi_d[:, 128 * j:128 * (j + 1)])
                        ldc[0] = sc.dmaT(tl_t[:], y1lo_d[:, 128 * j:128 * (j + 1)])

                    @block.vector
                    def _(e, j=j, ldc=ldc):
                        ve.wait_dma(sc, ldc[0])
                        ve.wait_dma(gp)
                        ve.e.tensor_tensor(fscr[:], th_t[:], tl_t[:], ALU.add)
                        ve.e.drain()
                        ve.e.tensor_reduce(tiny[:, 0:1], fscr[:], AX.X, ALU.max, apply_absolute_value=True)
                        ve.e.drain()
                        ve.e.tensor_scalar(tiny[:, 1:2], rstd[:, j:j + 1], C2048, None, ALU.mult)
                        ve.e.drain()
                        ve.e.tensor_tensor(tiny[:, 2:3], tiny[:, 1:2], tiny[:, 0:1], ALU.mult)
                        ve.e.drain()
                        ve.e.tensor_scalar(a1c[:, j:j + 1], tiny[:, 2:3], 1e-5, None, ALU.max)
                        ve.e.drain()
                        ve.e.reciprocal(tiny[:, 3:4], a1c[:, j:j + 1])
                        ve.e.drain()
                        ve.e.tensor_tensor(tiny[:, 4:5], tiny[:, 1:2], tiny[:, 3:4], ALU.mult)
                        ve.e.drain()
                        ve.inc(ve.e.tensor_scalar(tiny[:, 5:6], tiny[:, 4:5], 127.0, None, ALU.mult))
                        ve.e.drain()

                    @block.scalar
                    def _(e, j=j):
                        sc.wait(ve)
                        sc.inc(sc.e.activation(mscr[:], fscr[:], AF.Copy, bias=0.0, scale=tiny[:, 5:6]))

                    @block.vector
                    def _(e, j=j):
                        ve.wait(sc)
                        ve.e.tensor_copy(q1i[:], mscr[:])
                        ve.e.drain()
                        ve.inc(ve.e.tensor_scalar(q1b[:], q1i[:], 127.0, -128.0, ALU.min, ALU.max))

                    @block.gpsimd
                    def _(e, j=j):
                        gp.wait(ve)
                        gp.dma(q1_tm[128 * j:128 * (j + 1), :], q1b[:])

                barrier()
        _sxc.close()

        with ExitStack() as _sff:
            q1T = _sff.enter_context(nc.sbuf_tensor("q1T", [128, 16384], BF16))

            @block.scalar
            def _(e):
                if PH >= 5:
                    for a in range(16):
                        sc.dmaT(q1T[:, 1024 * a:1024 * (a + 1)], q1_tm[:, 128 * a:128 * (a + 1)])

            @block.vector
            def _(e):
                if PH < 5:
                    return
                for j in range(NT):
                    ve.e.tensor_scalar(tiny[:, 0:1], a1c[:, j:j + 1], 1.0 / 127.0, None, ALU.mult)
                    ve.e.drain()
                    ve.inc(ve.e.tensor_tensor(tiny[:, 8 + j:9 + j], tiny[:, 0:1], wsc_bc[:, 4:5], ALU.mult))
                    ve.e.drain()

            @block.gpsimd
            def _(e):
                if PH < 5:
                    return
                gp.wait(ve)
                for j in range(NT):
                    gp.dma(bass.AP(a_row_d.tensor, 128 * j, [[1, 128], [1, 1]]), tiny[:, 8 + j:9 + j])

            for ch in range(4 if PH >= 5 else 0):
                ffn_chunk(nc, block, sy, gp, sc, ve, te, ch,
                          q1T, h128t, wsc_bc, a1c, dsc, tiny,
                          wgT_all, wuT_all, wdT_all, a_row_d, y2hi_d, y2lo_d, q2_tm, acc_d,
                          slot_dma, guS, wdS)

            barrier()

        if PH >= 6:
            attn_phase(nc, block, sy, gp, sc, ve, te, ccS, cc_n,
                       killbt, dmt, ones_b,
                       q_tm, kT_own, v_own_d, kT_all, v_all, wo_in, linv_d, acc_d,
                       slot_dma, woS, kvS)

        if DEBUG_TAPS:
            tp = {}
            for nm, shape, dt_ in [("tp_xchi", [T, E], BF16), ("tp_q1", [T, E], BF16),
                                   ("tp_qtm", [T, E], BF16), ("tp_ktm", [T, E], BF16),
                                   ("tp_vown", [T, E], BF16), ("tp_tern", [1024, E], BF16),
                                   ("tp_acc", [T, E], F32), ("tp_rstd", [128, 8], F32),
                                   ("tp_wsc", [128, 8], F32), ("tp_lin", [1, NH * 512], F32),
                                   ("tp_a1c", [128, 8], F32), ("tp_dsc", [128, 8], F32)]:
                tp[nm] = nc.dram_tensor(nm, shape, dt_, kind="ExternalOutput").ap()

            barrier()

            @block.gpsimd
            def _(e):
                gp.dma(tp["tp_xchi"][:], xchi_d[:])
                gp.dma(tp["tp_q1"][:], q1_tm[:])
                gp.dma(tp["tp_qtm"][:], q_tm[:])
                gp.dma(tp["tp_ktm"][:], k_tm[:])
                gp.dma(tp["tp_vown"][:], v_own_d[:])
                gp.dma(tp["tp_tern"][:], wg_tern[:])
                gp.dma(tp["tp_acc"][:], acc_d[:])
                gp.dma(tp["tp_lin"][:], linv_d[:])

            @block.vector
            def _(e):
                ve.inc(ve.e.tensor_copy(tiny[:, 0:8], rstd[:]))

            @block.gpsimd
            def _(e):
                gp.wait(ve)
                gp.dma(tp["tp_rstd"][:], tiny[:, 0:8])
                gp.dma(tp["tp_wsc"][:], wsc_bc[:])
                gp.dma(tp["tp_a1c"][:], a1c[:])
                gp.dma(tp["tp_dsc"][:], dsc[:])

        with ExitStack() as _s7:
            sb7 = lambda *a: _s7.enter_context(nc.sbuf_tensor(*a))
            xa = sb7("xa", [128, E], F32)
            ac = sb7("ac", [128, E], F32)
            oo = sb7("oo", [128, E], F32)
            barrier()
            for j in range(NT):
                ld = [0]

                @block.sync
                def _(e, j=j, ld=ld):
                    sy.wait(ve)
                    sy.wait_dma(gp)
                    sy.dma(xa[:], x_in[128 * j:128 * (j + 1), :])
                    ld[0] = sy.dma(ac[:], acc_d[128 * j:128 * (j + 1), :])

                @block.vector
                def _(e, j=j, ld=ld):
                    ve.wait_dma(sy, ld[0])
                    ve.inc(ve.e.tensor_tensor(oo[:], ac[:], xa[:], ALU.add))

                @block.gpsimd
                def _(e, j=j):
                    gp.wait(ve)
                    gp.dma(out_d[128 * j:128 * (j + 1), :], oo[:])

            @block.gpsimd
            def _(e):
                gp.e.wait_ge(gQd, gp.dn)

    return nc


def barrier_nocc(ALL, sy, gp, sc, ve, te):
    for w in ALL:
        for o in (sy, gp, sc, ve):
            if o.dn > 0:
                w.e.wait_ge(o.dsem, o.dn)
        for o in (gp, sc, ve, te):
            if o.n > 0:
                w.e.wait_ge(o.sem, o.n)


def ffn_chunk(nc, block, sy, gp, sc, ve, te, ch,
              q1T, h128t, wsc_bc, a1c, dsc, tiny,
              wgT_all, wuT_all, wdT_all, a_row_d, y2hi_d, y2lo_d, q2_tm, acc_d,
              slot_dma, guS, wdS):
    F32_ = F32
    with ExitStack() as _sy1:
        sbc = lambda *a: _sy1.enter_context(nc.sbuf_tensor(*a))
        Y1 = sbc(f"Y1_{ch}", [128, 16384], F32)
        wgt = [sbc(f"wgt{ch}_{i}", [128, 2048], BF16) for i in range(2)]
        wut = [sbc(f"wut{ch}_{i}", [128, 2048], BF16) for i in range(2)]
        abc = sbc(f"abc_{ch}", [128, 256], F32)
        e1s = sbc(f"e1s_{ch}", [128, 256], F32)
        e2s = sbc(f"e2s_{ch}", [128, 256], F32)
        zh = sbc(f"zh_{ch}", [128, 256], BF16)
        zl = sbc(f"zl_{ch}", [128, 256], BF16)
        pg = [_sy1.enter_context(nc.psum_tensor(f"pg{ch}_{i}", [128, 256], F32)) for i in range(2)]
        pu = [_sy1.enter_context(nc.psum_tensor(f"pu{ch}_{i}", [128, 256], F32)) for i in range(2)]
        py = [_sy1.enter_context(nc.psum_tensor(f"py{ch}_{i}", [128, 256], F32)) for i in range(2)]

        @block.sync
        def _(e, ch=ch):
            sy.wait(ve)
            sy.wait_dma(gp)
            sy.dma(abc[:], bass.AP(a_row_d.tensor, 256 * ch, [[0, 128], [1, 256]]))

        for A in range(64):
            slot = A % 2
            slab, icol = A // 8, 128 * (A % 8)
            ldw = [0]

            @block.sync
            def _(e, A=A, slot=slot, slab=slab, icol=icol, ldw=ldw):
                if A >= 2:
                    sy.wait(te)
                elif A == 0:
                    sy.wait(te)
                    sy.wait(ve)
                off_g = slab * (2048 * 1024) + icol
                slot_dma(sy, guS[slot], wgt[slot][:], bass.AP(wgT_all.tensor, off_g,
                                                              [[1024, 128], [131072, 16], [1, 128]]))
                ldw[0] = slot_dma(sy, guS[slot], wut[slot][:], bass.AP(wuT_all.tensor, off_g,
                                                                      [[1024, 128], [131072, 16], [1, 128]]))

            @block.tensor
            def _(e, A=A, slot=slot, ch=ch, ldw=ldw):
                te.e.wait_ge(guS[slot], ldw[0])
                if A == 0:
                    te.wait_dma(sc)
                if A >= 2:
                    te.wait(ve)
                    te.wait(sc)
                for fc in range(16):
                    rhs = q1T[:, 1024 * fc + 256 * ch:1024 * fc + 256 * ch + 256]
                    te.e.matmul(pg[slot][:], wgt[slot][:, 128 * fc:128 * (fc + 1)], rhs,
                                start=(fc == 0), stop=(fc == 15))
                    mm = te.e.matmul(pu[slot][:], wut[slot][:, 128 * fc:128 * (fc + 1)], rhs,
                                     start=(fc == 0), stop=(fc == 15))
                    if fc == 15:
                        te.inc(mm)

            @block.vector
            def _(e, A=A, slot=slot):
                ve.wait(te)
                if A == 0:
                    ve.wait_dma(sy)
                ve.inc(ve.e.tensor_tensor(e1s[:], pg[slot][:], abc[:], ALU.mult))
                ve.e.drain()

            @block.scalar
            def _(e, A=A):
                sc.wait(ve)
                sc.inc(sc.e.activation(e2s[:], e1s[:], AF.Silu, bias=0.0, scale=1.0))

            @block.vector
            def _(e, A=A, slot=slot):
                ve.wait(sc)
                ve.e.tensor_tensor(e1s[:], e2s[:], pu[slot][:], ALU.mult)
                ve.e.drain()
                ve.e.tensor_copy(zh[:], e1s[:])
                ve.e.drain()
                ve.inc(ve.e.tensor_tensor(zl[:], e1s[:], zh[:], ALU.subtract))

            @block.tensor
            def _(e, A=A, slot=slot):
                te.wait(ve)
                te.e.matmul(py[slot][:], h128t[:], zh[:], start=True, stop=False)
                te.inc(te.e.matmul(py[slot][:], h128t[:], zl[:], start=False, stop=True))

            @block.scalar
            def _(e, A=A, slot=slot):
                sc.wait(te)
                sc.inc(sc.e.activation(Y1[:, 256 * A:256 * (A + 1)], py[slot][:],
                                       AF.Copy, bias=0.0, scale=1.0))

        @block.vector
        def _(e):
            ve.wait(sc)
            for st_ in range(6):
                bs = 1 << st_
                pairs = [(a, a + bs) for a in range(64) if (a // bs) % 2 == 0]
                for (pa, pb) in pairs:
                    ve.e.tensor_tensor(Y1[:, 256 * pa:256 * (pa + 1)],
                                       Y1[:, 256 * pa:256 * (pa + 1)],
                                       Y1[:, 256 * pb:256 * (pb + 1)], ALU.add)
                ve.e.drain()
                for pi, (pa, pb) in enumerate(pairs):
                    op = ve.e.scalar_tensor_tensor(Y1[:, 256 * pb:256 * (pb + 1)],
                                                   Y1[:, 256 * pb:256 * (pb + 1)], -2.0,
                                                   Y1[:, 256 * pa:256 * (pa + 1)], ALU.mult, ALU.add)
                    if st_ == 5 and pi == len(pairs) - 1:
                        ve.inc(op)
                ve.e.drain()

        for A in range(64):
            @block.vector
            def _(e, A=A):
                ve.wait_dma(gp)
                ve.e.tensor_copy(zh[:], Y1[:, 256 * A:256 * (A + 1)])
                ve.e.drain()
                ve.inc(ve.e.tensor_tensor(zl[:], Y1[:, 256 * A:256 * (A + 1)], zh[:], ALU.subtract))

            @block.gpsimd
            def _(e, A=A):
                gp.wait(ve)
                gp.dma(y2hi_d[128 * A:128 * (A + 1), :], zh[:])
                gp.dma(y2lo_d[128 * A:128 * (A + 1), :], zl[:])

    with ExitStack() as _sy2:
        sbq = lambda *a: _sy2.enter_context(nc.sbuf_tensor(*a))
        qh2 = sbq(f"qh2_{ch}", [128, II], BF16)
        ql2 = sbq(f"ql2_{ch}", [128, II], BF16)
        fs2 = sbq(f"fs2_{ch}", [128, II], F32)
        ms2 = sbq(f"ms2_{ch}", [128, II], F32)
        q2i = sbq(f"q2i_{ch}", [128, II], I32)
        q2b = sbq(f"q2b_{ch}", [128, II], BF16)

        for jj in range(2):
            jt = 2 * ch + jj
            ldc = [0]

            @block.scalar
            def _(e, jj=jj, ldc=ldc):
                sc.wait_dma(gp)
                sc.wait(ve)
                sc.dmaT(qh2[:], y2hi_d[:, 128 * jj:128 * (jj + 1)])
                ldc[0] = sc.dmaT(ql2[:], y2lo_d[:, 128 * jj:128 * (jj + 1)])

            @block.vector
            def _(e, jj=jj, jt=jt, ldc=ldc):
                ve.wait_dma(sc, ldc[0])
                ve.wait_dma(gp)
                ve.e.tensor_tensor(fs2[:], qh2[:], ql2[:], ALU.add)
                ve.e.drain()
                ve.e.tensor_reduce(tiny[:, 0:1], fs2[:], AX.X, ALU.max, apply_absolute_value=True)
                ve.e.drain()
                ve.e.tensor_scalar(tiny[:, 1:2], a1c[:, jt:jt + 1], 1.0 / 127.0, None, ALU.mult)
                ve.e.drain()
                ve.e.tensor_tensor(tiny[:, 2:3], tiny[:, 1:2], wsc_bc[:, 5:6], ALU.mult)
                ve.e.drain()
                ve.e.tensor_scalar(tiny[:, 3:4], tiny[:, 2:3], C8192, None, ALU.mult)
                ve.e.drain()
                ve.e.tensor_tensor(tiny[:, 4:5], tiny[:, 3:4], tiny[:, 0:1], ALU.mult)
                ve.e.drain()
                ve.e.tensor_scalar(tiny[:, 5:6], tiny[:, 4:5], 1e-5, None, ALU.max)
                ve.e.drain()
                ve.e.reciprocal(tiny[:, 6:7], tiny[:, 5:6])
                ve.e.drain()
                ve.e.tensor_tensor(tiny[:, 7:8], tiny[:, 3:4], tiny[:, 6:7], ALU.mult)
                ve.e.drain()
                ve.e.tensor_scalar(tiny[:, 8:9], tiny[:, 7:8], 127.0, None, ALU.mult)
                ve.e.drain()
                ve.e.tensor_scalar(tiny[:, 9:10], tiny[:, 5:6], 1.0 / 127.0, None, ALU.mult)
                ve.e.drain()
                ve.inc(ve.e.tensor_tensor(dsc[:, jt:jt + 1], tiny[:, 9:10], wsc_bc[:, 6:7], ALU.mult))
                ve.e.drain()

            @block.scalar
            def _(e, jj=jj):
                sc.wait(ve)
                sc.inc(sc.e.activation(ms2[:], fs2[:], AF.Copy, bias=0.0, scale=tiny[:, 8:9]))

            @block.vector
            def _(e, jj=jj):
                ve.wait(sc)
                ve.e.tensor_copy(q2i[:], ms2[:])
                ve.e.drain()
                ve.inc(ve.e.tensor_scalar(q2b[:], q2i[:], 127.0, -128.0, ALU.min, ALU.max))

            @block.gpsimd
            def _(e, jj=jj):
                gp.wait(ve)
                gp.dma(q2_tm[128 * jj:128 * (jj + 1), :], q2b[:])

    with ExitStack() as _sy3:
        sbd = lambda *a: _sy3.enter_context(nc.sbuf_tensor(*a))
        q2T = sbd(f"q2T_{ch}", [128, 16384], BF16)
        wdt = [sbd(f"wdt{ch}_{i}", [128, 512], BF16) for i in range(2)]
        dvs = sbd(f"dvs_{ch}", [128, 512], F32)
        pd = [_sy3.enter_context(nc.psum_tensor(f"pd{ch}_{i}", [128, 512], F32)) for i in range(4)]

        @block.scalar
        def _(e, ch=ch):
            sc.wait_dma(gp)
            for A in range(64):
                sc.dmaT(q2T[:, 256 * A:256 * (A + 1)], q2_tm[:, 128 * A:128 * (A + 1)])

        for og in range(4):
            for A in range(64):
                slot = A % 2
                slab, irow = A // 8, 128 * (A % 8)
                ldw = [0]

                @block.sync
                def _(e, og=og, A=A, slot=slot, slab=slab, irow=irow, ldw=ldw):
                    if og * 64 + A >= 2:
                        sy.wait(te)
                    elif og == 0 and A == 0:
                        sy.wait(te)
                        sy.wait(ve)
                    off = slab * (1024 * 2048) + irow * 2048 + 512 * og
                    ldw[0] = slot_dma(sy, wdS[slot], wdt[slot][:], bass.AP(wdT_all.tensor, off,
                                                                          [[2048, 128], [1, 512]]))

                @block.tensor
                def _(e, og=og, A=A, slot=slot, ldw=ldw):
                    te.e.wait_ge(wdS[slot], ldw[0])
                    if A == 0:
                        te.wait_dma(sc)
                        if og >= 2:
                            te.wait(ve)
                    bank = 2 * (og % 2)
                    for jj in range(2):
                        mm = te.e.matmul(pd[bank + jj][:],
                                         q2T[:, 256 * A + 128 * jj:256 * A + 128 * (jj + 1)],
                                         wdt[slot][:], start=(A == 0), stop=(A == 63))
                        if jj == 1:
                            te.inc(mm)

            for jj in range(2):
                jt = 2 * ch + jj

                @block.vector
                def _(e, og=og, jj=jj, jt=jt):
                    ve.wait(te)
                    ve.wait_dma(gp)
                    bank = 2 * (og % 2)
                    ve.inc(ve.e.tensor_scalar(dvs[:], pd[bank + jj][:], dsc[:, jt:jt + 1], None, ALU.mult))
                    ve.e.drain()

                @block.gpsimd
                def _(e, og=og, jj=jj, jt=jt):
                    gp.wait(ve)
                    gp.dma(acc_d[128 * jt:128 * (jt + 1), 512 * og:512 * (og + 1)], dvs[:])


def attn_phase(nc, block, sy, gp, sc, ve, te, ccS, cc_n,
               killbt, dmt, ones_b,
               q_tm, kT_own, v_own_d, kT_all, v_all, wo_in, linv_d, acc_d,
               slot_dma, woS, kvS):
    with ExitStack() as _sa:
        sba = lambda *a: _sa.enter_context(nc.sbuf_tensor(*a))
        kt2 = [sba(f"kt2_{i}", [128, 8192], BF16) for i in range(2)]
        vt2 = [sba(f"vt2_{i}", [128, 8192], BF16) for i in range(2)]
        qTt = sba("qTt", [128, 8192], BF16)
        PT = sba("PT", [128, 2048], BF16)
        opT = sba("opT", [128, 8192], F32)
        opn = sba("opn", [128, 8192], BF16)
        lacc = sba("lacc", [1, 8192], F32)
        lbc = sba("lbc", [128, 512], F32)
        wot = [sba(f"wot{i}", [128, 512], BF16) for i in range(2)]
        wsc2 = sba("wsc2", [128, 512], F32)
        pstE = _sa.enter_context(nc.psum_tensor("pstE", [128, 2048], F32))
        pstO = _sa.enter_context(nc.psum_tensor("pstO", [128, 2048], F32))

        for sect, w in ((block.sync, sy), (block.gpsimd, gp), (block.scalar, sc),
                        (block.vector, ve), (block.tensor, te)):
            def _f(e, w=w):
                w.e.wait_ge(ccS, cc_n[0])
            sect(_f)

        for qt in range(2):
            row0 = 512 if qt == 0 else 0
            niter = 16 if qt == 0 else 8
            kbase = 0 if qt == 0 else 16

            @block.scalar
            def _(e, qt=qt, row0=row0):
                sc.wait(ve)
                sc.wait(te)
                for h in range(NH):
                    sc.dmaT(qTt[:, 512 * h:512 * (h + 1)],
                            q_tm[row0:row0 + 512, 128 * h:128 * (h + 1)])

            for it in range(niter):
                buf = it % 2
                ldkv = [0]

                @block.gpsimd
                def _(e, qt=qt, it=it, buf=buf, row0=row0, ldkv=ldkv):
                    if it >= 2:
                        gp.wait(te)
                    if it == 0:
                        slot_dma(gp, kvS[buf], kt2[buf][:], bass.AP(kT_own.tensor, row0,
                                                                   [[1024, 128], [131072, 16], [1, 512]]))
                        ldkv[0] = slot_dma(gp, kvS[buf], vt2[buf][:], bass.AP(v_own_d.tensor, row0 * 2048,
                                                                             [[2048, 128], [262144, 4], [1, 2048]]))
                    else:
                        b = it - 1
                        slab = b if b <= 7 else 15 - b
                        half = 0 if b <= 7 else 1
                        koff = slab * (2048 * 1024) + half * 512
                        voff = slab * (1024 * 2048) + half * 512 * 2048
                        slot_dma(gp, kvS[buf], kt2[buf][:], bass.AP(kT_all.tensor, koff,
                                                                   [[1024, 128], [131072, 16], [1, 512]]))
                        ldkv[0] = slot_dma(gp, kvS[buf], vt2[buf][:], bass.AP(v_all.tensor, voff,
                                                                             [[2048, 128], [262144, 4], [1, 2048]]))

                for h in range(NH):
                    pst = pstE if h % 2 == 0 else pstO

                    @block.tensor
                    def _(e, qt=qt, it=it, h=h, buf=buf, pst=pst, ldkv=ldkv):
                        if h == 0:
                            te.e.wait_ge(kvS[buf], ldkv[0])
                            te.wait_dma(sc)
                        te.wait(ve)
                        for s_ in range(4):
                            mm = te.e.matmul(pst[:, 512 * s_:512 * (s_ + 1)],
                                             kt2[buf][:, 512 * h + 128 * s_:512 * h + 128 * (s_ + 1)],
                                             qTt[:, 512 * h:512 * (h + 1)], start=True, stop=True)
                            if s_ == 3:
                                te.inc(mm)

                    if it == 0:
                        @block.vector
                        def _(e, h=h, pst=pst):
                            ve.wait(te)
                            ve.inc(ve.e.tensor_tensor(pst[:], pst[:], dmt[:], ALU.add))
                            ve.e.drain()

                    @block.scalar
                    def _(e, qt=qt, it=it, h=h, pst=pst, kbase=kbase):
                        sc.wait(te)
                        if it == 0:
                            sc.wait(ve)
                        sc.inc(sc.e.activation(PT[:], pst[:], AF.Exp,
                                               bias=killbt[:, kbase + it:kbase + it + 1], scale=1.0))

                    @block.tensor
                    def _(e, h=h, it=it, buf=buf, pst=pst):
                        te.wait(sc)
                        for s_ in range(4):
                            te.e.matmul(pst[:, 0:512],
                                        vt2[buf][:, 2048 * s_ + 128 * h:2048 * s_ + 128 * (h + 1)],
                                        PT[:, 512 * s_:512 * (s_ + 1)], start=(s_ == 0), stop=(s_ == 3))
                        for s_ in range(4):
                            mm = te.e.matmul(pst[0:1, 512:1024], ones_b[:], PT[:, 512 * s_:512 * (s_ + 1)],
                                             start=(s_ == 0), stop=(s_ == 3))
                            if s_ == 3:
                                te.inc(mm)

                    @block.vector
                    def _(e, h=h, it=it, pst=pst):
                        ve.wait(te)
                        if it == 0:
                            ve.e.tensor_copy(opT[:, 512 * h:512 * (h + 1)], pst[:, 0:512])
                            ve.e.drain()
                            ve.inc(ve.e.tensor_copy(lacc[:, 512 * h:512 * (h + 1)], pst[0:1, 512:1024]))
                        else:
                            ve.e.tensor_tensor(opT[:, 512 * h:512 * (h + 1)],
                                               opT[:, 512 * h:512 * (h + 1)], pst[:, 0:512], ALU.add)
                            ve.e.drain()
                            ve.inc(ve.e.tensor_tensor(lacc[:, 512 * h:512 * (h + 1)],
                                                      lacc[:, 512 * h:512 * (h + 1)], pst[0:1, 512:1024], ALU.add))
                        ve.e.drain()

            @block.vector
            def _(e):
                ve.inc(ve.e.reciprocal(lacc[:], lacc[:]))
                ve.e.drain()

            @block.gpsimd
            def _(e):
                gp.wait(ve)
                gp.dma(linv_d[:], lacc[:])

            for h in range(NH):
                ldc = [0]

                @block.sync
                def _(e, h=h, ldc=ldc):
                    sy.wait_dma(gp)
                    sy.wait(ve)
                    ldc[0] = sy.dma(lbc[:], bass.AP(linv_d.tensor, 512 * h, [[0, 128], [1, 512]]))

                @block.vector
                def _(e, h=h, ldc=ldc):
                    ve.wait_dma(sy, ldc[0])
                    ve.inc(ve.e.tensor_tensor(opn[:, 512 * h:512 * (h + 1)],
                                       opT[:, 512 * h:512 * (h + 1)], lbc[:], ALU.mult))
                    ve.e.drain()

            if True:
                pw = [pstE[:, 512 * i:512 * (i + 1)] for i in range(4)]
                for eg in range(4):
                    for h in range(NH):
                        slot = h % 2
                        ldw = [0]

                        @block.gpsimd
                        def _(e, eg=eg, h=h, slot=slot, ldw=ldw):
                            if eg * NH + h >= 2:
                                gp.wait(te)
                            ldw[0] = slot_dma(gp, woS[slot], wot[slot][:],
                                              wo_in[128 * h:128 * (h + 1), 512 * eg:512 * (eg + 1)])

                        @block.tensor
                        def _(e, eg=eg, h=h, slot=slot, ldw=ldw):
                            te.e.wait_ge(woS[slot], ldw[0])
                            if h == 0:
                                te.wait(ve)
                            for qs_ in range(4):
                                mm = te.e.matmul(pw[qs_],
                                                 opn[:, 512 * h + 128 * qs_:512 * h + 128 * (qs_ + 1)],
                                                 wot[slot][:], start=(h == 0), stop=(h == NH - 1))
                                if qs_ == 3:
                                    te.inc(mm)

                    for qs_ in range(4):
                        @block.vector
                        def _(e, eg=eg, qs_=qs_):
                            ve.wait(te)
                            ve.wait_dma(gp)
                            ve.inc(ve.e.tensor_copy(wsc2[:], pw[qs_]))
                            ve.e.drain()

                        @block.gpsimd
                        def _(e, eg=eg, qs_=qs_, qt=qt):
                            gp.wait(ve)
                            row = (512 if qt == 0 else 0) + 128 * qs_
                            gp.dma(acc_d[row:row + 128, 512 * eg:512 * (eg + 1)],
                                   wsc2[:], accum_op=ALU.add)


if __name__ == "__main__":
    pass


# version 02
# speedup vs baseline: 1.0486x; 1.0486x over previous
import math
import os
import numpy as np
import ml_dtypes
import concourse.bass as bass
import concourse.mybir as mybir
from concourse.bass_utils import run_bass_kernel_spmd
from concourse.replica_groups import maybe_share_collective_output_space
from contextlib import ExitStack

F32 = mybir.dt.float32
I32 = mybir.dt.int32
BF16 = mybir.dt.bfloat16
AF = mybir.ActivationFunctionType
ALU = mybir.AluOpType
AX = mybir.AxisListType
bfdt = ml_dtypes.bfloat16

NCORE = 8
S, E, NH, HD, II = 8192, 2048, 16, 128, 8192
T = 1024
NT = 8
RG = [list(range(NCORE))]
CSHIFT = 14.0
KILL = -30000.0
C2048 = float(1.0 / math.sqrt(2048.0))
C8192 = float(1.0 / math.sqrt(8192.0))
WCNT = float(8192 * 2048)

DEBUG_TAPS = False
TAP_LIST = ()
PH = 9
P3S = 9
KOG = 12
KEVAC = 1
KF = int(os.environ.get("KF", "16"))
KNWR = int(os.environ.get("KNWR", "4"))
PH = 9
P3S = 9
KOG = 12
KEVAC = 1
KF = int(os.environ.get("KF", "16"))
KNWR = int(os.environ.get("KNWR", "4"))


def hadamard(n):
    h = np.array([[1.0]], dtype=np.float64)
    while h.shape[0] < n:
        h = np.block([[h, h], [h, -h]])
    return h


class Eng:
    def __init__(self, e, sem, dma_sem=None):
        self.e = e
        self.sem = sem
        self.dsem = dma_sem
        self.n = 0
        self.dn = 0

    def dma(self, out, in_, **kw):
        self.e.dma_start(out=out, in_=in_, **kw).then_inc(self.dsem, 16)
        self.dn += 16
        return self.dn

    def dmaT(self, out, in_):
        self.e.dma_start_transpose(out, in_).then_inc(self.dsem, 16)
        self.dn += 16
        return self.dn

    def inc(self, instr):
        instr.then_inc(self.sem, 1)
        self.n += 1
        return self.n

    def wait(self, other, val=None):
        v = other.n if val is None else val
        if v > 0:
            self.e.wait_ge(other.sem, v)

    def wait_dma(self, other, val=None):
        v = other.dn if val is None else val
        if v > 0:
            self.e.wait_ge(other.dsem, v)


def make_inmaps(L0_attn_x, L0_attn_Wqkv, L0_attn_Wo, L0_ffn_w_gate, L0_ffn_w_up, L0_ffn_w_down):
    x = np.ascontiguousarray(np.asarray(L0_attn_x, dtype=np.float32).reshape(S, E))
    wqkv = np.ascontiguousarray(np.asarray(L0_attn_Wqkv, dtype=np.float32))
    wo = np.ascontiguousarray(np.asarray(L0_attn_Wo, dtype=np.float32))
    wg = np.asarray(L0_ffn_w_gate, dtype=np.float32)
    wu = np.asarray(L0_ffn_w_up, dtype=np.float32)
    wd = np.asarray(L0_ffn_w_down, dtype=np.float32)

    h128 = hadamard(128).astype(bfdt)
    i = np.arange(HD // 2, dtype=np.float32)
    inv_freq = (1.0 / 10000.0 ** (2.0 * i / HD)).astype(np.float32)
    pos = np.arange(S, dtype=np.float32)
    th = np.outer(pos, inv_freq).astype(np.float32)
    cosf = np.cos(th).astype(np.float32)
    sinf = np.sin(th).astype(np.float32)
    qs = np.float32(1.0 / math.sqrt(HD))
    kk = np.arange(512)
    dmask = np.where(kk[:, None] <= kk[None, :], 0.0, KILL).astype(bfdt)

    in_maps = []
    for c in range(NCORE):
        rows_b = slice(512 * c, 512 * c + 512)
        rows_a = slice(8192 - 512 * (c + 1), 8192 - 512 * c)
        own = np.concatenate([x[rows_b], x[rows_a]], axis=0)
        cos_own = np.concatenate([cosf[rows_b], cosf[rows_a]], axis=0)
        sin_own = np.concatenate([sinf[rows_b], sinf[rows_a]], axis=0)
        kb = np.full((32,), -CSHIFT, np.float32)
        for t_ in range(1, 16):
            if t_ - 1 >= 15 - c:
                kb[t_] = KILL
        for t_ in range(1, 8):
            if t_ - 1 >= c:
                kb[16 + t_] = KILL
        kb[24:] = KILL
        in_maps.append({
            "x_own": own,
            "wqkv": wqkv,
            "wo": wo,
            "wg_sh": np.ascontiguousarray(wg[1024 * c:1024 * (c + 1)]),
            "wu_sh": np.ascontiguousarray(wu[1024 * c:1024 * (c + 1)]),
            "wd_sh": np.ascontiguousarray(wd[:, 1024 * c:1024 * (c + 1)]),
            "h128": h128,
            "cosq": (cos_own * qs).astype(np.float32),
            "sinq": (sin_own * qs).astype(np.float32),
            "cosk": cos_own,
            "sink": sin_own,
            "killb": np.broadcast_to(kb, (128, 32)).copy(),
            "dmask": dmask,
        })

    return in_maps


def kernel(**inputs):
    in_maps = make_inmaps(**inputs)
    nc = build_nc()
    ncr = NCORE
    res = run_bass_kernel_spmd(nc, in_maps, core_ids=list(range(NCORE)))

    out = np.zeros((S, E), np.float32)
    for c in range(ncr):
        r = res.results[c]["out_own"]
        out[512 * c:512 * c + 512] = r[0:512]
        out[8192 - 512 * (c + 1):8192 - 512 * c] = r[512:1024]
    kernel.taps = res.results
    return out.reshape(1, S, E)


def build_nc():
    nc = bass.Bass("TRN2", target_bir_lowering=False)

    x_in = nc.dram_tensor("x_own", [T, E], F32, kind="ExternalInput").ap()
    wqkv_in = nc.dram_tensor("wqkv", [E, 3 * E], F32, kind="ExternalInput").ap()
    wo_in = nc.dram_tensor("wo", [E, E], F32, kind="ExternalInput").ap()
    wg_in = nc.dram_tensor("wg_sh", [1024, E], F32, kind="ExternalInput").ap()
    wu_in = nc.dram_tensor("wu_sh", [1024, E], F32, kind="ExternalInput").ap()
    wd_in = nc.dram_tensor("wd_sh", [E, 1024], F32, kind="ExternalInput").ap()
    h128_in = nc.dram_tensor("h128", [128, 128], BF16, kind="ExternalInput").ap()
    cosq_in = nc.dram_tensor("cosq", [T, 64], F32, kind="ExternalInput").ap()
    sinq_in = nc.dram_tensor("sinq", [T, 64], F32, kind="ExternalInput").ap()
    cosk_in = nc.dram_tensor("cosk", [T, 64], F32, kind="ExternalInput").ap()
    sink_in = nc.dram_tensor("sink", [T, 64], F32, kind="ExternalInput").ap()
    killb_in = nc.dram_tensor("killb", [128, 32], F32, kind="ExternalInput").ap()
    dmask_in = nc.dram_tensor("dmask", [512, 512], BF16, kind="ExternalInput").ap()
    out_d = nc.dram_tensor("out_own", [T, E], F32, kind="ExternalOutput").ap()
    tp_xchS = nc.dram_tensor("tp_xchS", [T, E], BF16, kind="ExternalOutput").ap() if DEBUG_TAPS else None

    taps = {}

    def tap(name, shape, dtype=F32):
        if DEBUG_TAPS and name in TAP_LIST:
            taps[name] = nc.dram_tensor(name, shape, dtype, kind="ExternalOutput").ap()
            return taps[name]
        return None

    xchi_d = nc.dram_tensor("xchi_d", [T, E], BF16).ap()
    xclo_d = nc.dram_tensor("xclo_d", [T, E], BF16).ap()
    q_tm = nc.dram_tensor("q_tm", [T, E], BF16).ap()
    k_tm = nc.dram_tensor("k_tm", [T, E], BF16).ap()
    kT_own = nc.dram_tensor("kT_own", [E, T], BF16).ap()
    v_own_d = nc.dram_tensor("v_own", [T, E], BF16).ap()
    sp_ag = maybe_share_collective_output_space("AllGather", RG)

    kT_all = nc.dram_tensor("kT_all", [NCORE, E, T], BF16, addr_space=sp_ag).ap()
    v_all = nc.dram_tensor("v_all", [NCORE, T, E], BF16, addr_space=sp_ag).ap()
    wg_tern = nc.dram_tensor("wg_tern", [1024, E], BF16).ap()
    wu_tern = nc.dram_tensor("wu_tern", [1024, E], BF16).ap()
    wd_tern = nc.dram_tensor("wd_tern", [E, 1024], BF16).ap()
    wgT_own = nc.dram_tensor("wgT_own", [E, 1024], BF16).ap()
    wuT_own = nc.dram_tensor("wuT_own", [E, 1024], BF16).ap()
    wdT_own = nc.dram_tensor("wdT_own", [1024, E], BF16).ap()
    wgT_all = nc.dram_tensor("wgT_all", [NCORE, E, 1024], BF16, addr_space=sp_ag).ap()
    wuT_all = nc.dram_tensor("wuT_all", [NCORE, E, 1024], BF16, addr_space=sp_ag).ap()
    wdT_all = nc.dram_tensor("wdT_all", [NCORE, 1024, E], BF16, addr_space=sp_ag).ap()
    sp_ar = maybe_share_collective_output_space("AllReduce", RG)

    ws_in = nc.dram_tensor("ws_in", [1, 8], F32).ap()
    ws_out = nc.dram_tensor("ws_out", [1, 8], F32, addr_space=sp_ar).ap()
    part_d = nc.dram_tensor("part_d", [4, 128], F32).ap()
    scal_d = nc.dram_tensor("scal_d", [1, 64], F32).ap()
    y1hi_d = nc.dram_tensor("y1hi_d", [E, T], BF16).ap()
    y1lo_d = nc.dram_tensor("y1lo_d", [E, T], BF16).ap()
    q1_tm = nc.dram_tensor("q1_tm", [T, E], BF16).ap()
    a_row_d = nc.dram_tensor("a_row_d", [1, T], F32).ap()
    y2hi_d = nc.dram_tensor("y2hi_d", [II, 256], BF16).ap()
    y2lo_d = nc.dram_tensor("y2lo_d", [II, 256], BF16).ap()
    q2_tm = nc.dram_tensor("q2_tm", [256, II], BF16).ap()
    acc_d = nc.dram_tensor("acc_d", [T, E], F32).ap()
    linv_d = nc.dram_tensor("linv_d", [1, NH * 512], F32).ap()

    with ExitStack() as _st:
        sb = lambda *a: _st.enter_context(nc.sbuf_tensor(*a))
        sem = lambda n: _st.enter_context(nc.semaphore(n))
        ones_b = sb("ones_b", [128, 1], BF16)
        h128t = sb("h128t", [128, 128], BF16)
        killbt = sb("killbt", [128, 32], F32)
        dmt = sb("dmt", [128, 2048], BF16)
        cosqt = sb("cosqt", [128, 512], F32)
        sinqt = sb("sinqt", [128, 512], F32)
        coskt = sb("coskt", [128, 512], F32)
        sinkt = sb("sinkt", [128, 512], F32)
        rstd = sb("rstd", [128, 8], F32)
        a1c = sb("a1c", [128, 8], F32)
        dsc = sb("dsc", [128, 8], F32)
        wsc_bc = sb("wsc_bc", [128, 8], F32)
        tiny = sb("tiny", [128, 16], F32)
        sQd = sem("sQ"); gQd = sem("gQ"); aQd = sem("aQ"); vQd = sem("vQ")
        tS = sem("tS"); vS = sem("vS"); sS = sem("sS"); gS = sem("gS"); ccS = sem("ccS")
        wrS = [sem(f"wrS{i}") for i in range(4)]
        guS = [sem(f"guS{i}") for i in range(2)]
        wdS = [sem(f"wdS{i}") for i in range(2)]
        woS = [sem(f"woS{i}") for i in range(2)]
        kvS = [sem(f"kvS{i}") for i in range(2)]
        slotcnt = {}

        def slot_dma(eng, semh, out, in_, **kw):
            eng.e.dma_start(out=out, in_=in_, **kw).then_inc(semh, 16)
            slotcnt[id(semh)] = slotcnt.get(id(semh), 0) + 16
            return slotcnt[id(semh)]
        block = _st.enter_context(nc.Block())

        sy = Eng(nc.sync, None, sQd)
        gp = Eng(nc.gpsimd, gS, gQd)
        sc = Eng(nc.scalar, sS, aQd)
        ve = Eng(nc.vector, vS, vQd)
        te = Eng(nc.tensor, tS, None)
        cc_n = [0]
        ALL = [sy, gp, sc, ve, te]

        def cc(kind, op, ins, outs):
            gp.e.collective_compute(kind, op, replica_groups=RG, ins=ins, outs=outs).then_inc(ccS, 1)
            cc_n[0] += 1
            return cc_n[0]

        def barrier(ccv=None):
            snap_d = [(o, o.dn) for o in (sy, gp, sc, ve)]
            snap_n = [(o, o.n) for o in (gp, sc, ve, te)]
            ccv = cc_n[0] if ccv is None else ccv
            for sect, w in ((block.sync, sy), (block.gpsimd, gp), (block.scalar, sc),
                            (block.vector, ve), (block.tensor, te)):
                def _f(e, w=w):
                    for o, v in snap_d:
                        if v > 0:
                            w.e.wait_ge(o.dsem, v)
                    for o, v in snap_n:
                        if v > 0:
                            w.e.wait_ge(o.sem, v)
                    if ccv > 0:
                        w.e.wait_ge(ccS, ccv)
                sect(_f)

        @block.sync
        def _(e):
            sy.dma(h128t[:], h128_in[:])
            sy.dma(killbt[:], killb_in[:])
            sy.dma(dmt[:], bass.AP(dmask_in.tensor, 0, [[512, 128], [65536, 4], [1, 512]]))
            for j in range(NT):
                sy.dma(cosqt[:, 64 * j:64 * (j + 1)], cosq_in[128 * j:128 * (j + 1), :])
                sy.dma(sinqt[:, 64 * j:64 * (j + 1)], sinq_in[128 * j:128 * (j + 1), :])
                sy.dma(coskt[:, 64 * j:64 * (j + 1)], cosk_in[128 * j:128 * (j + 1), :])
                sy.dma(sinkt[:, 64 * j:64 * (j + 1)], sink_in[128 * j:128 * (j + 1), :])

        @block.vector
        def _(e):
            ve.inc(ve.e.memset(ones_b[:], 1.0))

        with ExitStack() as _s1:
            sb1 = lambda *a: _s1.enter_context(nc.sbuf_tensor(*a))
            xt = sb1("xt", [128, E], F32)
            sqs = sb1("sqs", [128, E], F32)
            xc = sb1("xc", [128, E], F32)
            xch = sb1("xch", [128, E], BF16)
            xcl = sb1("xcl", [128, E], BF16)

            for j in range(NT):
                ld = [0]

                @block.sync
                def _(e, j=j, ld=ld):
                    sy.wait(sc)
                    sy.wait(ve)
                    ld[0] = sy.dma(xt[:], x_in[128 * j:128 * (j + 1), :])

                @block.scalar
                def _(e, j=j, ld=ld):
                    sc.wait_dma(sy, ld[0])
                    sc.inc(sc.e.activation(sqs[:], xt[:], AF.Square, bias=0.0, scale=1.0,
                                           accum_out=tiny[:, 0:1]))

                @block.vector
                def _(e, j=j, ld=ld):
                    ve.wait_dma(sy, ld[0])
                    ve.e.tensor_reduce(tiny[:, 5:6], xt[:], AX.X, ALU.add)
                    ve.e.drain()
                    ve.e.tensor_scalar(tiny[:, 1:2], tiny[:, 5:6], 1.0 / E, None, ALU.mult)
                    ve.wait(sc)
                    ve.e.tensor_scalar(tiny[:, 6:7], tiny[:, 0:1], 1.0 / E, None, ALU.mult)
                    ve.e.drain()
                    ve.e.tensor_tensor(tiny[:, 3:4], tiny[:, 1:2], tiny[:, 1:2], ALU.mult)
                    ve.e.drain()
                    ve.e.tensor_tensor(tiny[:, 7:8], tiny[:, 6:7], tiny[:, 3:4], ALU.subtract)
                    ve.e.drain()
                    ve.e.tensor_scalar(tiny[:, 2:3], tiny[:, 7:8], 1e-5, None, ALU.add)
                    ve.e.drain()
                    ve.inc(ve.e.reciprocal(tiny[:, 4:5], tiny[:, 2:3]))
                    ve.e.drain()

                @block.scalar
                def _(e, j=j):
                    sc.wait(ve)
                    sc.inc(sc.e.activation(tiny[:, 8:9], tiny[:, 4:5], AF.Sqrt, bias=0.0, scale=1.0))

                @block.vector
                def _(e, j=j):
                    ve.wait(sc)
                    for _it in range(2):
                        ve.e.tensor_tensor(tiny[:, 9:10], tiny[:, 8:9], tiny[:, 8:9], ALU.mult)
                        ve.e.drain()
                        ve.e.tensor_tensor(tiny[:, 10:11], tiny[:, 9:10], tiny[:, 2:3], ALU.mult)
                        ve.e.drain()
                        ve.e.tensor_scalar(tiny[:, 11:12], tiny[:, 10:11], -0.5, 1.5, ALU.mult, ALU.add)
                        ve.e.drain()
                        ve.e.tensor_tensor(tiny[:, 8:9], tiny[:, 8:9], tiny[:, 11:12], ALU.mult)
                        ve.e.drain()
                    ve.e.tensor_copy(rstd[:, j:j + 1], tiny[:, 8:9])
                    ve.e.drain()
                    ve.wait_dma(gp)
                    ve.e.tensor_scalar(xc[:], xt[:], tiny[:, 1:2], None, ALU.subtract)
                    ve.e.drain()
                    ve.e.tensor_copy(xch[:], xc[:])
                    ve.e.drain()
                    ve.inc(ve.e.tensor_tensor(xcl[:], xc[:], xch[:], ALU.subtract))

                @block.gpsimd
                def _(e, j=j):
                    gp.wait(ve)
                    gp.dma(xchi_d[128 * j:128 * (j + 1), :], xch[:])
                    gp.dma(xclo_d[128 * j:128 * (j + 1), :], xcl[:])
                    if DEBUG_TAPS:
                        gp.dma(tp_xchS[128 * j:128 * (j + 1), :], xch[:])


            barrier()

        _sxc = ExitStack()
        XCHI_T = _sxc.enter_context(nc.sbuf_tensor("XCHI_T", [128, 16384], BF16))
        XCLO_T = _sxc.enter_context(nc.sbuf_tensor("XCLO_T", [128, 16384], BF16))

        @block.scalar
        def _(e):
            for a in range(16):
                sc.dmaT(XCHI_T[:, 1024 * a:1024 * (a + 1)], xchi_d[:, 128 * a:128 * (a + 1)])
                sc.dmaT(XCLO_T[:, 1024 * a:1024 * (a + 1)], xclo_d[:, 128 * a:128 * (a + 1)])

        if PH >= 2:
            with ExitStack() as _s2:
                sb2 = lambda *a: _s2.enter_context(nc.sbuf_tensor(*a))
                wt = sb2("wt", [128, E], F32)
                wtern = sb2("wtern", [128, E], BF16)
                wqi = sb2("wqi", [128, E], I32)
                prow = sb2("prow", [1, 128], F32)
                srow = sb2("srow", [1, 8], F32)
                wtt = sb2("wtt", [128, E], BF16)

                @block.vector
                def _(e):
                    ve.inc(ve.e.memset(srow[:], 0.0))

                mats = [(wg_in, 8, E, wg_tern), (wu_in, 8, E, wu_tern), (wd_in, 16, 1024, wd_tern)]
                for mi, (w_in, ntile, tw, _w) in enumerate(mats):
                    for t_ in range(ntile):
                        ld = [0]

                        @block.sync
                        def _(e, mi=mi, t_=t_, w_in=w_in, tw=tw, ld=ld):
                            sy.wait(ve)
                            ld[0] = sy.dma(wt[:, 0:tw], w_in[128 * t_:128 * (t_ + 1), :])

                        @block.vector
                        def _(e, mi=mi, t_=t_, tw=tw, ld=ld, first=(t_ == 0)):
                            ve.wait_dma(sy, ld[0])
                            ve.e.tensor_reduce(tiny[:, 0:1], wt[:, 0:tw], AX.X, ALU.add,
                                               apply_absolute_value=True)
                            ve.e.drain()
                            if first:
                                ve.inc(ve.e.tensor_copy(tiny[:, 8 + mi:9 + mi], tiny[:, 0:1]))
                            else:
                                ve.inc(ve.e.tensor_tensor(tiny[:, 8 + mi:9 + mi],
                                                          tiny[:, 8 + mi:9 + mi], tiny[:, 0:1], ALU.add))
                            ve.e.drain()

                @block.gpsimd
                def _(e):
                    gp.wait(ve)
                    for mi in range(3):
                        gp.dma(bass.AP(part_d.tensor, 128 * mi, [[1, 128], [1, 1]]), tiny[:, 8 + mi:9 + mi])

                for mi in range(3):
                    ld = [0]

                    @block.sync
                    def _(e, mi=mi, ld=ld):
                        sy.wait(ve)
                        sy.wait_dma(gp)
                        ld[0] = sy.dma(prow[:, :], bass.AP(part_d.tensor, 128 * mi, [[0, 1], [1, 128]]))

                    @block.vector
                    def _(e, mi=mi, ld=ld):
                        ve.wait_dma(sy, ld[0])
                        ve.inc(ve.e.tensor_reduce(srow[:, mi:mi + 1], prow[:], AX.X, ALU.add))
                        ve.e.drain()

                @block.gpsimd
                def _(e):
                    gp.wait(ve)
                    gp.dma(ws_in[:], srow[:])
                    gp.e.wait_ge(gQd, gp.dn)
                    cc("AllReduce", ALU.add, [ws_in[:]], [ws_out[:]])

                @block.sync
                def _(e):
                    sy.e.wait_ge(ccS, cc_n[0])
                    sy.dma(srow[:], ws_out[:])

                @block.vector
                def _(e):
                    ve.wait_dma(sy)
                    ve.e.tensor_scalar(srow[:, 4:8], srow[:, 0:4], 1.0 / WCNT, None, ALU.mult)
                    ve.e.drain()
                    ve.e.tensor_scalar(srow[:, 4:8], srow[:, 4:8], 1e-5, None, ALU.max)
                    ve.e.drain()
                    ve.inc(ve.e.reciprocal(srow[:, 0:4], srow[:, 4:8]))
                    ve.e.drain()

                @block.gpsimd
                def _(e):
                    gp.wait(ve)
                    gp.dma(bass.AP(scal_d.tensor, 0, [[0, 1], [1, 8]]), srow[:, 0:8])

                @block.sync
                def _(e):
                    sy.wait_dma(gp)
                    sy.dma(wsc_bc[:, 0:8], bass.AP(scal_d.tensor, 0, [[0, 128], [1, 8]]))

                for mi, (w_in, ntile, tw, wtern_d) in enumerate(mats):
                    for t_ in range(ntile):
                        ld = [0]

                        @block.sync
                        def _(e, mi=mi, t_=t_, w_in=w_in, tw=tw, ld=ld):
                            sy.wait(ve)
                            ld[0] = sy.dma(wt[:, 0:tw], w_in[128 * t_:128 * (t_ + 1), :])

                        @block.vector
                        def _(e, mi=mi, t_=t_, tw=tw, ld=ld):
                            ve.wait_dma(sy)
                            ve.wait_dma(gp)
                            ve.e.tensor_scalar(wt[:, 0:tw], wt[:, 0:tw], wsc_bc[:, mi:mi + 1], None, ALU.mult)
                            ve.e.drain()
                            ve.e.tensor_copy(wqi[:, 0:tw], wt[:, 0:tw])
                            ve.e.drain()
                            ve.inc(ve.e.tensor_scalar(wtern[:, 0:tw], wqi[:, 0:tw], 1.0, -1.0, ALU.min, ALU.max))

                        @block.gpsimd
                        def _(e, mi=mi, t_=t_, tw=tw, wtern_d=wtern_d):
                            gp.wait(ve)
                            gp.dma(wtern_d[128 * t_:128 * (t_ + 1), 0:tw], wtern[:, 0:tw])

                def transpose_out(src_d, dst_d, ncolblk, srows):
                    for a in range(ncolblk):
                        ldc = [0]

                        @block.scalar
                        def _(e, a=a, ldc=ldc):
                            sc.wait_dma(gp)
                            ldc[0] = sc.dmaT(wtt[:, 0:srows], src_d[:, 128 * a:128 * (a + 1)])

                        @block.gpsimd
                        def _(e, a=a, ldc=ldc):
                            gp.wait_dma(sc, ldc[0])
                            gp.dma(dst_d[128 * a:128 * (a + 1), 0:srows], wtt[:, 0:srows])

                transpose_out(wg_tern, wgT_own, 16, 1024)
                transpose_out(wu_tern, wuT_own, 16, 1024)
                transpose_out(wd_tern, wdT_own, 8, 2048)

                @block.gpsimd
                def _(e):
                    gp.e.wait_ge(gQd, gp.dn)
                    cc("AllGather", ALU.bypass, [wgT_own[:]], [wgT_all[:]])
                    cc("AllGather", ALU.bypass, [wuT_own[:]], [wuT_all[:]])
                    cc("AllGather", ALU.bypass, [wdT_own[:]], [wdT_all[:]])

                barrier(ccv=0)
        AGW_CC = cc_n[0]

        if PH >= 3:
            with ExitStack() as _s3:
                sb3 = lambda *a: _s3.enter_context(nc.sbuf_tensor(*a))
                NWR = 4
                wring = [sb3(f"wr{i}", [128, 512], BF16) for i in range(NWR)]
                crq = sb3("crq", [128, 2048], F32)
                srq = sb3("srq", [128, 2048], F32)
                crk = sb3("crk", [128, 2048], F32)
                srk = sb3("srk", [128, 2048], F32)
                rs1 = sb3("rs1", [128, 256], F32)
                rs2 = sb3("rs2", [128, 256], F32)
                rs3 = sb3("rs3", [128, 256], F32)
                rs4 = sb3("rs4", [128, 256], F32)
                ro = sb3("ro", [128, 4096], F32)
                ktt2 = sb3("ktt2", [128, 1024], BF16)
                ps8 = [_s3.enter_context(nc.psum_tensor(f"pq{i}", [128, 512], F32)) for i in range(8)]

                @block.vector
                def _(e):
                    for j in range(NT):
                        for r in range(4):
                            cs_ = slice(256 * j + 64 * r, 256 * j + 64 * (r + 1))
                            js = slice(64 * j, 64 * (j + 1))
                            ve.e.tensor_scalar(crq[:, cs_], cosqt[:, js], rstd[:, j:j + 1], None, ALU.mult)
                            ve.e.tensor_scalar(srq[:, cs_], sinqt[:, js], rstd[:, j:j + 1], None, ALU.mult)
                            ve.e.tensor_scalar(crk[:, cs_], coskt[:, js], rstd[:, j:j + 1], None, ALU.mult)
                            ve.inc(ve.e.tensor_scalar(srk[:, cs_], sinkt[:, js], rstd[:, j:j + 1], None, ALU.mult))
                    ve.e.drain()

                qk_ck = []
                for og in range((KOG if P3S >= 2 else 0)):
                    for grp in range(2):
                        gi = og * 2 + grp
                        jlist = [4 * grp + jj for jj in range(4)]

                        te_base = te.n
                        gp_base = gp.dn

                        wr_tgt = []

                        @block.gpsimd
                        def _(e, og=og, grp=grp, te_base=te_base, wr_tgt=wr_tgt):
                            for f in range(16):
                                need = te_base - (NWR - 1) + f
                                if need > 0:
                                    gp.e.wait_ge(te.sem, need)
                                wr_tgt.append(slot_dma(gp, wrS[f % NWR], wring[f % NWR][:],
                                                       wqkv_in[128 * f:128 * (f + 1), 512 * og:512 * (og + 1)]))

                        @block.tensor
                        def _(e, og=og, grp=grp, gi=gi, jlist=jlist, gp_base=gp_base, wr_tgt=wr_tgt):
                            if gi >= 2:
                                te.wait(ve, qk_ck[gi - 2])
                            if gi == 0:
                                te.wait(ve)
                                te.wait_dma(sc)
                            for f in range(16):
                                te.e.wait_ge(wrS[f % NWR], wr_tgt[f])
                                for j in jlist:
                                    mm = te.e.matmul(ps8[j][:],
                                                     XCHI_T[:, 1024 * f + 128 * j:1024 * f + 128 * (j + 1)],
                                                     wring[f % NWR][:], start=(f == 0), stop=(f == 15))
                                    if j == jlist[-1]:
                                        te.inc(mm)

                        if KEVAC == 0:
                            @block.vector
                            def _(e):
                                ve.wait(te)
                                for j in jlist:
                                    cp = ve.e.tensor_copy(ro[:, 512 * j:512 * (j + 1)], ps8[j][:])
                                    if j == jlist[-1]:
                                        ve.inc(cp)
                                ve.e.drain()
                            qk_ck.append(ve.n)
                        elif og < 8:
                            ctab, stab = (crq, srq) if og < 4 else (crk, srk)
                            dst = q_tm if og < 4 else k_tm
                            col = 512 * (og % 4)

                            @block.vector
                            def _(e, og=og, jlist=jlist, ctab=ctab, stab=stab):
                                ve.wait(te)
                                ve.wait_dma(gp)
                                for j in jlist:
                                    ps = ps8[j]
                                    cj = slice(256 * j, 256 * (j + 1))
                                    ev = bass.AP(ps, 0, [[512, 128], [2, 256]])
                                    od = bass.AP(ps, 1, [[512, 128], [2, 256]])
                                    ve.e.tensor_tensor(rs1[:], ev, ctab[:, cj], ALU.mult)
                                    ve.e.tensor_tensor(rs2[:], od, stab[:, cj], ALU.mult)
                                    ve.e.tensor_tensor(rs3[:], ev, stab[:, cj], ALU.mult)
                                    ve.e.tensor_tensor(rs4[:], od, ctab[:, cj], ALU.mult)
                                    ve.e.drain()
                                    roe = bass.AP(ro, 512 * j, [[4096, 128], [2, 256]])
                                    roo = bass.AP(ro, 512 * j + 1, [[4096, 128], [2, 256]])
                                    ve.e.tensor_tensor(roe, rs1[:], rs2[:], ALU.subtract)
                                    ve.inc(ve.e.tensor_tensor(roo, rs3[:], rs4[:], ALU.add))
                                    ve.e.drain()
                            qk_ck.append(ve.n)

                            @block.gpsimd
                            def _(e, jlist=jlist, dst=dst, col=col):
                                gp.wait(ve)
                                for j in jlist:
                                    gp.dma(dst[128 * j:128 * (j + 1), col:col + 512],
                                           ro[:, 512 * j:512 * (j + 1)])
                        else:
                            col = 512 * (og - 8)

                            @block.vector
                            def _(e, jlist=jlist):
                                ve.wait(te)
                                ve.wait_dma(gp)
                                for j in jlist:
                                    op = ve.e.tensor_scalar(ro[:, 512 * j:512 * (j + 1)], ps8[j][:],
                                                            rstd[:, j:j + 1], None, ALU.mult)
                                    if j == jlist[-1]:
                                        ve.inc(op)
                                ve.e.drain()
                            qk_ck.append(ve.n)

                            @block.gpsimd
                            def _(e, jlist=jlist, col=col):
                                gp.wait(ve)
                                for j in jlist:
                                    gp.dma(v_own_d[128 * j:128 * (j + 1), col:col + 512],
                                           ro[:, 512 * j:512 * (j + 1)])

                for a in range(16 if P3S >= 3 else 0):
                    ldc = [0]

                    @block.scalar
                    def _(e, a=a, ldc=ldc):
                        sc.wait_dma(gp)
                        ldc[0] = sc.dmaT(ktt2[:], k_tm[:, 128 * a:128 * (a + 1)])

                    @block.gpsimd
                    def _(e, a=a, ldc=ldc):
                        gp.wait_dma(sc, ldc[0])
                        gp.dma(kT_own[128 * a:128 * (a + 1), :], ktt2[:])

                @block.gpsimd
                def _(e):
                    gp.e.wait_ge(gQd, gp.dn)
                    if P3S >= 4:
                        cc("AllGather", ALU.bypass, [kT_own[:]], [kT_all[:]])
                        cc("AllGather", ALU.bypass, [v_own_d[:]], [v_all[:]])

                barrier(ccv=AGW_CC)

        if PH >= 4:
            with ExitStack() as _s4:
                sb4 = lambda *a: _s4.enter_context(nc.sbuf_tensor(*a))
                F1 = sb4("F1", [128, 16384], F32)
                fh = sb4("fh", [128, 1024], BF16)
                fl = sb4("fl", [128, 1024], BF16)
                th_t = sb4("th_t", [128, 2048], BF16)
                tl_t = sb4("tl_t", [128, 2048], BF16)
                fscr = sb4("fscr", [128, 2048], F32)
                mscr = sb4("mscr", [128, 2048], F32)
                q1i = sb4("q1i", [128, 2048], I32)
                q1b = sb4("q1b", [128, 2048], BF16)
                pf = [_s4.enter_context(nc.psum_tensor(f"pf{i}", [128, 512], F32)) for i in range(2)]

                for a in range(16):
                    for half in range(2):
                        colsT = slice(1024 * a + 512 * half, 1024 * a + 512 * (half + 1))
                        idx = (a * 2 + half) % 2

                        @block.tensor
                        def _(e, colsT=colsT, idx=idx, first=(a == 0 and half == 0)):
                            te.wait(sc)
                            if first:
                                te.wait_dma(sc)
                            te.e.matmul(pf[idx][:], h128t[:], XCHI_T[:, colsT], start=True, stop=False)
                            te.inc(te.e.matmul(pf[idx][:], h128t[:], XCLO_T[:, colsT], start=False, stop=True))

                        @block.scalar
                        def _(e, colsT=colsT, idx=idx):
                            sc.wait(te)
                            sc.inc(sc.e.activation(F1[:, colsT], pf[idx][:], AF.Copy, bias=0.0, scale=1.0))

                @block.vector
                def _(e):
                    ve.wait(sc)
                    for st_ in range(4):
                        bs = 1 << st_
                        pairs = [(a, a + bs) for a in range(16) if (a // bs) % 2 == 0]
                        for (pa, pb) in pairs:
                            ve.e.tensor_tensor(F1[:, 1024 * pa:1024 * (pa + 1)],
                                               F1[:, 1024 * pa:1024 * (pa + 1)],
                                               F1[:, 1024 * pb:1024 * (pb + 1)], ALU.add)
                        ve.e.drain()
                        for pi, (pa, pb) in enumerate(pairs):
                            op = ve.e.scalar_tensor_tensor(F1[:, 1024 * pb:1024 * (pb + 1)],
                                                           F1[:, 1024 * pb:1024 * (pb + 1)], -2.0,
                                                           F1[:, 1024 * pa:1024 * (pa + 1)], ALU.mult, ALU.add)
                            if st_ == 3 and pi == len(pairs) - 1:
                                ve.inc(op)
                        ve.e.drain()

                for a in range(16):
                    @block.vector
                    def _(e, a=a):
                        ve.wait_dma(gp)
                        cols = slice(1024 * a, 1024 * (a + 1))
                        ve.e.tensor_copy(fh[:], F1[:, cols])
                        ve.e.drain()
                        ve.inc(ve.e.tensor_tensor(fl[:], F1[:, cols], fh[:], ALU.subtract))

                    @block.gpsimd
                    def _(e, a=a):
                        gp.wait(ve)
                        gp.dma(y1hi_d[128 * a:128 * (a + 1), :], fh[:])
                        gp.dma(y1lo_d[128 * a:128 * (a + 1), :], fl[:])

                for j in range(NT):
                    ldc = [0]

                    @block.scalar
                    def _(e, j=j, ldc=ldc):
                        sc.wait_dma(gp)
                        sc.wait(ve)
                        sc.dmaT(th_t[:], y1hi_d[:, 128 * j:128 * (j + 1)])
                        ldc[0] = sc.dmaT(tl_t[:], y1lo_d[:, 128 * j:128 * (j + 1)])

                    @block.vector
                    def _(e, j=j, ldc=ldc):
                        ve.wait_dma(sc, ldc[0])
                        ve.wait_dma(gp)
                        ve.e.tensor_tensor(fscr[:], th_t[:], tl_t[:], ALU.add)
                        ve.e.drain()
                        ve.e.tensor_reduce(tiny[:, 0:1], fscr[:], AX.X, ALU.max, apply_absolute_value=True)
                        ve.e.drain()
                        ve.e.tensor_scalar(tiny[:, 1:2], rstd[:, j:j + 1], C2048, None, ALU.mult)
                        ve.e.drain()
                        ve.e.tensor_tensor(tiny[:, 2:3], tiny[:, 1:2], tiny[:, 0:1], ALU.mult)
                        ve.e.drain()
                        ve.e.tensor_scalar(a1c[:, j:j + 1], tiny[:, 2:3], 1e-5, None, ALU.max)
                        ve.e.drain()
                        ve.e.reciprocal(tiny[:, 3:4], a1c[:, j:j + 1])
                        ve.e.drain()
                        ve.e.tensor_tensor(tiny[:, 4:5], tiny[:, 1:2], tiny[:, 3:4], ALU.mult)
                        ve.e.drain()
                        ve.inc(ve.e.tensor_scalar(tiny[:, 5:6], tiny[:, 4:5], 127.0, None, ALU.mult))
                        ve.e.drain()

                    @block.scalar
                    def _(e, j=j):
                        sc.wait(ve)
                        sc.inc(sc.e.activation(mscr[:], fscr[:], AF.Copy, bias=0.0, scale=tiny[:, 5:6]))

                    @block.vector
                    def _(e, j=j):
                        ve.wait(sc)
                        ve.e.tensor_copy(q1i[:], mscr[:])
                        ve.e.drain()
                        ve.inc(ve.e.tensor_scalar(q1b[:], q1i[:], 127.0, -128.0, ALU.min, ALU.max))

                    @block.gpsimd
                    def _(e, j=j):
                        gp.wait(ve)
                        gp.dma(q1_tm[128 * j:128 * (j + 1), :], q1b[:])

                barrier()
        _sxc.close()

        with ExitStack() as _sff:
            q1T = _sff.enter_context(nc.sbuf_tensor("q1T", [128, 16384], BF16))

            @block.scalar
            def _(e):
                if PH >= 5:
                    for a in range(16):
                        sc.dmaT(q1T[:, 1024 * a:1024 * (a + 1)], q1_tm[:, 128 * a:128 * (a + 1)])

            @block.vector
            def _(e):
                if PH < 5:
                    return
                for j in range(NT):
                    ve.e.tensor_scalar(tiny[:, 0:1], a1c[:, j:j + 1], 1.0 / 127.0, None, ALU.mult)
                    ve.e.drain()
                    ve.inc(ve.e.tensor_tensor(tiny[:, 8 + j:9 + j], tiny[:, 0:1], wsc_bc[:, 4:5], ALU.mult))
                    ve.e.drain()

            @block.gpsimd
            def _(e):
                if PH < 5:
                    return
                gp.wait(ve)
                for j in range(NT):
                    gp.dma(bass.AP(a_row_d.tensor, 128 * j, [[1, 128], [1, 1]]), tiny[:, 8 + j:9 + j])

            for ch in range(4 if PH >= 5 else 0):
                ffn_chunk(nc, block, sy, gp, sc, ve, te, ch,
                          q1T, h128t, wsc_bc, a1c, dsc, tiny,
                          wgT_all, wuT_all, wdT_all, a_row_d, y2hi_d, y2lo_d, q2_tm, acc_d,
                          slot_dma, guS, wdS)

            barrier()

        if PH >= 6:
            attn_phase(nc, block, sy, gp, sc, ve, te, ccS, cc_n,
                       killbt, dmt, ones_b,
                       q_tm, kT_own, v_own_d, kT_all, v_all, wo_in, linv_d, acc_d,
                       slot_dma, woS, kvS)

        if DEBUG_TAPS:
            tp = {}
            for nm, shape, dt_ in [("tp_xchi", [T, E], BF16), ("tp_q1", [T, E], BF16),
                                   ("tp_qtm", [T, E], BF16), ("tp_ktm", [T, E], BF16),
                                   ("tp_vown", [T, E], BF16), ("tp_tern", [1024, E], BF16),
                                   ("tp_acc", [T, E], F32), ("tp_rstd", [128, 8], F32),
                                   ("tp_wsc", [128, 8], F32), ("tp_lin", [1, NH * 512], F32),
                                   ("tp_a1c", [128, 8], F32), ("tp_dsc", [128, 8], F32)]:
                tp[nm] = nc.dram_tensor(nm, shape, dt_, kind="ExternalOutput").ap()

            barrier()

            @block.gpsimd
            def _(e):
                gp.dma(tp["tp_xchi"][:], xchi_d[:])
                gp.dma(tp["tp_q1"][:], q1_tm[:])
                gp.dma(tp["tp_qtm"][:], q_tm[:])
                gp.dma(tp["tp_ktm"][:], k_tm[:])
                gp.dma(tp["tp_vown"][:], v_own_d[:])
                gp.dma(tp["tp_tern"][:], wg_tern[:])
                gp.dma(tp["tp_acc"][:], acc_d[:])
                gp.dma(tp["tp_lin"][:], linv_d[:])

            @block.vector
            def _(e):
                ve.inc(ve.e.tensor_copy(tiny[:, 0:8], rstd[:]))

            @block.gpsimd
            def _(e):
                gp.wait(ve)
                gp.dma(tp["tp_rstd"][:], tiny[:, 0:8])
                gp.dma(tp["tp_wsc"][:], wsc_bc[:])
                gp.dma(tp["tp_a1c"][:], a1c[:])
                gp.dma(tp["tp_dsc"][:], dsc[:])

        with ExitStack() as _s7:
            sb7 = lambda *a: _s7.enter_context(nc.sbuf_tensor(*a))
            xa = sb7("xa", [128, E], F32)
            ac = sb7("ac", [128, E], F32)
            oo = sb7("oo", [128, E], F32)
            barrier()
            for j in range(NT):
                ld = [0]

                @block.sync
                def _(e, j=j, ld=ld):
                    sy.wait(ve)
                    sy.wait_dma(gp)
                    sy.dma(xa[:], x_in[128 * j:128 * (j + 1), :])
                    ld[0] = sy.dma(ac[:], acc_d[128 * j:128 * (j + 1), :])

                @block.vector
                def _(e, j=j, ld=ld):
                    ve.wait_dma(sy, ld[0])
                    ve.inc(ve.e.tensor_tensor(oo[:], ac[:], xa[:], ALU.add))

                @block.gpsimd
                def _(e, j=j):
                    gp.wait(ve)
                    gp.dma(out_d[128 * j:128 * (j + 1), :], oo[:])

            @block.gpsimd
            def _(e):
                gp.e.wait_ge(gQd, gp.dn)

    return nc


def barrier_nocc(ALL, sy, gp, sc, ve, te):
    for w in ALL:
        for o in (sy, gp, sc, ve):
            if o.dn > 0:
                w.e.wait_ge(o.dsem, o.dn)
        for o in (gp, sc, ve, te):
            if o.n > 0:
                w.e.wait_ge(o.sem, o.n)


def ffn_chunk(nc, block, sy, gp, sc, ve, te, ch,
              q1T, h128t, wsc_bc, a1c, dsc, tiny,
              wgT_all, wuT_all, wdT_all, a_row_d, y2hi_d, y2lo_d, q2_tm, acc_d,
              slot_dma, guS, wdS):
    F32_ = F32
    with ExitStack() as _sy1:
        sbc = lambda *a: _sy1.enter_context(nc.sbuf_tensor(*a))
        Y1 = sbc(f"Y1_{ch}", [128, 16384], F32)
        wgt = [sbc(f"wgt{ch}_{i}", [128, 2048], BF16) for i in range(2)]
        wut = [sbc(f"wut{ch}_{i}", [128, 2048], BF16) for i in range(2)]
        abc = sbc(f"abc_{ch}", [128, 256], F32)
        e1s = sbc(f"e1s_{ch}", [128, 256], F32)
        e2s = sbc(f"e2s_{ch}", [128, 256], F32)
        zh = sbc(f"zh_{ch}", [128, 256], BF16)
        zl = sbc(f"zl_{ch}", [128, 256], BF16)
        pg = [_sy1.enter_context(nc.psum_tensor(f"pg{ch}_{i}", [128, 256], F32)) for i in range(2)]
        pu = [_sy1.enter_context(nc.psum_tensor(f"pu{ch}_{i}", [128, 256], F32)) for i in range(2)]
        py = [_sy1.enter_context(nc.psum_tensor(f"py{ch}_{i}", [128, 256], F32)) for i in range(2)]

        @block.sync
        def _(e, ch=ch):
            sy.wait(ve)
            sy.wait_dma(gp)
            sy.dma(abc[:], bass.AP(a_row_d.tensor, 256 * ch, [[0, 128], [1, 256]]))

        vz_ck = []
        y1_ck = []
        for A in range(64):
            slot = A % 2
            slab, icol = A // 8, 128 * (A % 8)
            ldw = [0]

            @block.sync
            def _(e, A=A, slot=slot, slab=slab, icol=icol, ldw=ldw):
                if A >= 2:
                    sy.wait(te)
                elif A == 0:
                    sy.wait(te)
                    sy.wait(ve)
                off_g = slab * (2048 * 1024) + icol
                slot_dma(sy, guS[slot], wgt[slot][:], bass.AP(wgT_all.tensor, off_g,
                                                              [[1024, 128], [131072, 16], [1, 128]]))
                ldw[0] = slot_dma(sy, guS[slot], wut[slot][:], bass.AP(wuT_all.tensor, off_g,
                                                                      [[1024, 128], [131072, 16], [1, 128]]))

            @block.tensor
            def _(e, A=A, slot=slot, ch=ch, ldw=ldw):
                te.e.wait_ge(guS[slot], ldw[0])
                if A == 0:
                    te.wait_dma(sc)
                if A >= 2:
                    te.wait(ve, vz_ck[A - 2])
                    te.wait(sc, y1_ck[A - 2])
                for fc in range(16):
                    rhs = q1T[:, 1024 * fc + 256 * ch:1024 * fc + 256 * ch + 256]
                    te.e.matmul(pg[slot][:], wgt[slot][:, 128 * fc:128 * (fc + 1)], rhs,
                                start=(fc == 0), stop=(fc == 15))
                    mm = te.e.matmul(pu[slot][:], wut[slot][:, 128 * fc:128 * (fc + 1)], rhs,
                                     start=(fc == 0), stop=(fc == 15))
                    if fc == 15:
                        te.inc(mm)

            @block.vector
            def _(e, A=A, slot=slot):
                ve.wait(te)
                if A == 0:
                    ve.wait_dma(sy)
                ve.inc(ve.e.tensor_tensor(e1s[:], pg[slot][:], abc[:], ALU.mult))
                ve.e.drain()

            @block.scalar
            def _(e, A=A):
                sc.wait(ve)
                sc.inc(sc.e.activation(e2s[:], e1s[:], AF.Silu, bias=0.0, scale=1.0))

            @block.vector
            def _(e, A=A, slot=slot):
                ve.wait(sc)
                ve.e.tensor_tensor(e1s[:], e2s[:], pu[slot][:], ALU.mult)
                ve.e.drain()
                ve.e.tensor_copy(zh[:], e1s[:])
                ve.e.drain()
                ve.inc(ve.e.tensor_tensor(zl[:], e1s[:], zh[:], ALU.subtract))
            vz_ck.append(ve.n)

            @block.tensor
            def _(e, A=A, slot=slot):
                te.wait(ve)
                te.e.matmul(py[slot][:], h128t[:], zh[:], start=True, stop=False)
                te.inc(te.e.matmul(py[slot][:], h128t[:], zl[:], start=False, stop=True))

            @block.scalar
            def _(e, A=A, slot=slot):
                sc.wait(te)
                sc.inc(sc.e.activation(Y1[:, 256 * A:256 * (A + 1)], py[slot][:],
                                       AF.Copy, bias=0.0, scale=1.0))
            y1_ck.append(sc.n)

        @block.vector
        def _(e):
            ve.wait(sc)
            for st_ in range(6):
                bs = 1 << st_
                pairs = [(a, a + bs) for a in range(64) if (a // bs) % 2 == 0]
                for (pa, pb) in pairs:
                    ve.e.tensor_tensor(Y1[:, 256 * pa:256 * (pa + 1)],
                                       Y1[:, 256 * pa:256 * (pa + 1)],
                                       Y1[:, 256 * pb:256 * (pb + 1)], ALU.add)
                ve.e.drain()
                for pi, (pa, pb) in enumerate(pairs):
                    op = ve.e.scalar_tensor_tensor(Y1[:, 256 * pb:256 * (pb + 1)],
                                                   Y1[:, 256 * pb:256 * (pb + 1)], -2.0,
                                                   Y1[:, 256 * pa:256 * (pa + 1)], ALU.mult, ALU.add)
                    if st_ == 5 and pi == len(pairs) - 1:
                        ve.inc(op)
                ve.e.drain()

        for A in range(64):
            @block.vector
            def _(e, A=A):
                ve.wait_dma(gp)
                ve.e.tensor_copy(zh[:], Y1[:, 256 * A:256 * (A + 1)])
                ve.e.drain()
                ve.inc(ve.e.tensor_tensor(zl[:], Y1[:, 256 * A:256 * (A + 1)], zh[:], ALU.subtract))

            @block.gpsimd
            def _(e, A=A):
                gp.wait(ve)
                gp.dma(y2hi_d[128 * A:128 * (A + 1), :], zh[:])
                gp.dma(y2lo_d[128 * A:128 * (A + 1), :], zl[:])

    with ExitStack() as _sy2:
        sbq = lambda *a: _sy2.enter_context(nc.sbuf_tensor(*a))
        qh2 = sbq(f"qh2_{ch}", [128, II], BF16)
        ql2 = sbq(f"ql2_{ch}", [128, II], BF16)
        fs2 = sbq(f"fs2_{ch}", [128, II], F32)
        ms2 = sbq(f"ms2_{ch}", [128, II], F32)
        q2i = sbq(f"q2i_{ch}", [128, II], I32)
        q2b = sbq(f"q2b_{ch}", [128, II], BF16)

        for jj in range(2):
            jt = 2 * ch + jj
            ldc = [0]

            @block.scalar
            def _(e, jj=jj, ldc=ldc):
                sc.wait_dma(gp)
                sc.wait(ve)
                sc.dmaT(qh2[:], y2hi_d[:, 128 * jj:128 * (jj + 1)])
                ldc[0] = sc.dmaT(ql2[:], y2lo_d[:, 128 * jj:128 * (jj + 1)])

            @block.vector
            def _(e, jj=jj, jt=jt, ldc=ldc):
                ve.wait_dma(sc, ldc[0])
                ve.wait_dma(gp)
                ve.e.tensor_tensor(fs2[:], qh2[:], ql2[:], ALU.add)
                ve.e.drain()
                ve.e.tensor_reduce(tiny[:, 0:1], fs2[:], AX.X, ALU.max, apply_absolute_value=True)
                ve.e.drain()
                ve.e.tensor_scalar(tiny[:, 1:2], a1c[:, jt:jt + 1], 1.0 / 127.0, None, ALU.mult)
                ve.e.drain()
                ve.e.tensor_tensor(tiny[:, 2:3], tiny[:, 1:2], wsc_bc[:, 5:6], ALU.mult)
                ve.e.drain()
                ve.e.tensor_scalar(tiny[:, 3:4], tiny[:, 2:3], C8192, None, ALU.mult)
                ve.e.drain()
                ve.e.tensor_tensor(tiny[:, 4:5], tiny[:, 3:4], tiny[:, 0:1], ALU.mult)
                ve.e.drain()
                ve.e.tensor_scalar(tiny[:, 5:6], tiny[:, 4:5], 1e-5, None, ALU.max)
                ve.e.drain()
                ve.e.reciprocal(tiny[:, 6:7], tiny[:, 5:6])
                ve.e.drain()
                ve.e.tensor_tensor(tiny[:, 7:8], tiny[:, 3:4], tiny[:, 6:7], ALU.mult)
                ve.e.drain()
                ve.e.tensor_scalar(tiny[:, 8:9], tiny[:, 7:8], 127.0, None, ALU.mult)
                ve.e.drain()
                ve.e.tensor_scalar(tiny[:, 9:10], tiny[:, 5:6], 1.0 / 127.0, None, ALU.mult)
                ve.e.drain()
                ve.inc(ve.e.tensor_tensor(dsc[:, jt:jt + 1], tiny[:, 9:10], wsc_bc[:, 6:7], ALU.mult))
                ve.e.drain()

            @block.scalar
            def _(e, jj=jj):
                sc.wait(ve)
                sc.inc(sc.e.activation(ms2[:], fs2[:], AF.Copy, bias=0.0, scale=tiny[:, 8:9]))

            @block.vector
            def _(e, jj=jj):
                ve.wait(sc)
                ve.e.tensor_copy(q2i[:], ms2[:])
                ve.e.drain()
                ve.inc(ve.e.tensor_scalar(q2b[:], q2i[:], 127.0, -128.0, ALU.min, ALU.max))

            @block.gpsimd
            def _(e, jj=jj):
                gp.wait(ve)
                gp.dma(q2_tm[128 * jj:128 * (jj + 1), :], q2b[:])

    with ExitStack() as _sy3:
        sbd = lambda *a: _sy3.enter_context(nc.sbuf_tensor(*a))
        q2T = sbd(f"q2T_{ch}", [128, 16384], BF16)
        wdt = [sbd(f"wdt{ch}_{i}", [128, 512], BF16) for i in range(2)]
        dvs = sbd(f"dvs_{ch}", [128, 512], F32)
        pd = [_sy3.enter_context(nc.psum_tensor(f"pd{ch}_{i}", [128, 512], F32)) for i in range(4)]

        @block.scalar
        def _(e, ch=ch):
            sc.wait_dma(gp)
            for A in range(64):
                sc.dmaT(q2T[:, 256 * A:256 * (A + 1)], q2_tm[:, 128 * A:128 * (A + 1)])

        for og in range(4):
            for A in range(64):
                slot = A % 2
                slab, irow = A // 8, 128 * (A % 8)
                ldw = [0]

                @block.sync
                def _(e, og=og, A=A, slot=slot, slab=slab, irow=irow, ldw=ldw):
                    if og * 64 + A >= 2:
                        sy.wait(te)
                    elif og == 0 and A == 0:
                        sy.wait(te)
                        sy.wait(ve)
                    off = slab * (1024 * 2048) + irow * 2048 + 512 * og
                    ldw[0] = slot_dma(sy, wdS[slot], wdt[slot][:], bass.AP(wdT_all.tensor, off,
                                                                          [[2048, 128], [1, 512]]))

                @block.tensor
                def _(e, og=og, A=A, slot=slot, ldw=ldw):
                    te.e.wait_ge(wdS[slot], ldw[0])
                    if A == 0:
                        te.wait_dma(sc)
                        if og >= 2:
                            te.wait(ve)
                    bank = 2 * (og % 2)
                    for jj in range(2):
                        mm = te.e.matmul(pd[bank + jj][:],
                                         q2T[:, 256 * A + 128 * jj:256 * A + 128 * (jj + 1)],
                                         wdt[slot][:], start=(A == 0), stop=(A == 63))
                        if jj == 1:
                            te.inc(mm)

            for jj in range(2):
                jt = 2 * ch + jj

                @block.vector
                def _(e, og=og, jj=jj, jt=jt):
                    ve.wait(te)
                    ve.wait_dma(gp)
                    bank = 2 * (og % 2)
                    ve.inc(ve.e.tensor_scalar(dvs[:], pd[bank + jj][:], dsc[:, jt:jt + 1], None, ALU.mult))
                    ve.e.drain()

                @block.gpsimd
                def _(e, og=og, jj=jj, jt=jt):
                    gp.wait(ve)
                    gp.dma(acc_d[128 * jt:128 * (jt + 1), 512 * og:512 * (og + 1)], dvs[:])


def attn_phase(nc, block, sy, gp, sc, ve, te, ccS, cc_n,
               killbt, dmt, ones_b,
               q_tm, kT_own, v_own_d, kT_all, v_all, wo_in, linv_d, acc_d,
               slot_dma, woS, kvS):
    with ExitStack() as _sa:
        sba = lambda *a: _sa.enter_context(nc.sbuf_tensor(*a))
        kt2 = [sba(f"kt2_{i}", [128, 8192], BF16) for i in range(2)]
        vt2 = [sba(f"vt2_{i}", [128, 8192], BF16) for i in range(2)]
        qTt = sba("qTt", [128, 8192], BF16)
        PT = sba("PT", [128, 2048], BF16)
        opT = sba("opT", [128, 8192], F32)
        opn = sba("opn", [128, 8192], BF16)
        lacc = sba("lacc", [1, 8192], F32)
        lbc = sba("lbc", [128, 512], F32)
        wot = [sba(f"wot{i}", [128, 512], BF16) for i in range(2)]
        wsc2 = sba("wsc2", [128, 512], F32)
        pstE = _sa.enter_context(nc.psum_tensor("pstE", [128, 2048], F32))
        pstO = _sa.enter_context(nc.psum_tensor("pstO", [128, 2048], F32))

        for sect, w in ((block.sync, sy), (block.gpsimd, gp), (block.scalar, sc),
                        (block.vector, ve), (block.tensor, te)):
            def _f(e, w=w):
                w.e.wait_ge(ccS, cc_n[0])
            sect(_f)

        for qt in range(2):
            row0 = 512 if qt == 0 else 0
            niter = 16 if qt == 0 else 8
            kbase = 0 if qt == 0 else 16

            @block.scalar
            def _(e, qt=qt, row0=row0):
                sc.wait(ve)
                sc.wait(te)
                for h in range(NH):
                    sc.dmaT(qTt[:, 512 * h:512 * (h + 1)],
                            q_tm[row0:row0 + 512, 128 * h:128 * (h + 1)])

            add_ck = []
            kvte_ck = []
            for it in range(niter):
                buf = it % 2
                ldkv = [0]

                @block.gpsimd
                def _(e, qt=qt, it=it, buf=buf, row0=row0, ldkv=ldkv):
                    if it >= 2:
                        gp.wait(te, kvte_ck[it - 2])
                    if it == 0:
                        slot_dma(gp, kvS[buf], kt2[buf][:], bass.AP(kT_own.tensor, row0,
                                                                   [[1024, 128], [131072, 16], [1, 512]]))
                        ldkv[0] = slot_dma(gp, kvS[buf], vt2[buf][:], bass.AP(v_own_d.tensor, row0 * 2048,
                                                                             [[2048, 128], [262144, 4], [1, 2048]]))
                    else:
                        b = it - 1
                        slab = b if b <= 7 else 15 - b
                        half = 0 if b <= 7 else 1
                        koff = slab * (2048 * 1024) + half * 512
                        voff = slab * (1024 * 2048) + half * 512 * 2048
                        slot_dma(gp, kvS[buf], kt2[buf][:], bass.AP(kT_all.tensor, koff,
                                                                   [[1024, 128], [131072, 16], [1, 512]]))
                        ldkv[0] = slot_dma(gp, kvS[buf], vt2[buf][:], bass.AP(v_all.tensor, voff,
                                                                             [[2048, 128], [262144, 4], [1, 2048]]))

                for h in range(NH):
                    pst = pstE if h % 2 == 0 else pstO

                    @block.tensor
                    def _(e, qt=qt, it=it, h=h, buf=buf, pst=pst, ldkv=ldkv, si=len(add_ck)):
                        if h == 0:
                            te.e.wait_ge(kvS[buf], ldkv[0])
                            te.wait_dma(sc)
                        if si >= 2:
                            te.wait(ve, add_ck[si - 2])
                        elif si == 0:
                            te.wait(ve)
                        for s_ in range(4):
                            mm = te.e.matmul(pst[:, 512 * s_:512 * (s_ + 1)],
                                             kt2[buf][:, 512 * h + 128 * s_:512 * h + 128 * (s_ + 1)],
                                             qTt[:, 512 * h:512 * (h + 1)], start=True, stop=True)
                            if s_ == 3:
                                te.inc(mm)

                    if it == 0:
                        @block.vector
                        def _(e, h=h, pst=pst):
                            ve.wait(te)
                            ve.inc(ve.e.tensor_tensor(pst[:], pst[:], dmt[:], ALU.add))
                            ve.e.drain()

                    @block.scalar
                    def _(e, qt=qt, it=it, h=h, pst=pst, kbase=kbase):
                        sc.wait(te)
                        if it == 0:
                            sc.wait(ve)
                        sc.inc(sc.e.activation(PT[:], pst[:], AF.Exp,
                                               bias=killbt[:, kbase + it:kbase + it + 1], scale=1.0))

                    @block.tensor
                    def _(e, h=h, it=it, buf=buf, pst=pst):
                        te.wait(sc)
                        for s_ in range(4):
                            te.e.matmul(pst[:, 0:512],
                                        vt2[buf][:, 2048 * s_ + 128 * h:2048 * s_ + 128 * (h + 1)],
                                        PT[:, 512 * s_:512 * (s_ + 1)], start=(s_ == 0), stop=(s_ == 3))
                        for s_ in range(4):
                            mm = te.e.matmul(pst[0:1, 512:1024], ones_b[:], PT[:, 512 * s_:512 * (s_ + 1)],
                                             start=(s_ == 0), stop=(s_ == 3))
                            if s_ == 3:
                                te.inc(mm)

                    @block.vector
                    def _(e, h=h, it=it, pst=pst):
                        ve.wait(te)
                        if it == 0:
                            ve.e.tensor_copy(opT[:, 512 * h:512 * (h + 1)], pst[:, 0:512])
                            ve.e.drain()
                            ve.inc(ve.e.tensor_copy(lacc[:, 512 * h:512 * (h + 1)], pst[0:1, 512:1024]))
                        else:
                            ve.e.tensor_tensor(opT[:, 512 * h:512 * (h + 1)],
                                               opT[:, 512 * h:512 * (h + 1)], pst[:, 0:512], ALU.add)
                            ve.e.drain()
                            ve.inc(ve.e.tensor_tensor(lacc[:, 512 * h:512 * (h + 1)],
                                                      lacc[:, 512 * h:512 * (h + 1)], pst[0:1, 512:1024], ALU.add))
                        ve.e.drain()
                    add_ck.append(ve.n)
                kvte_ck.append(te.n)

            @block.vector
            def _(e):
                ve.inc(ve.e.reciprocal(lacc[:], lacc[:]))
                ve.e.drain()

            @block.gpsimd
            def _(e):
                gp.wait(ve)
                gp.dma(linv_d[:], lacc[:])

            for h in range(NH):
                ldc = [0]

                @block.sync
                def _(e, h=h, ldc=ldc):
                    sy.wait_dma(gp)
                    sy.wait(ve)
                    ldc[0] = sy.dma(lbc[:], bass.AP(linv_d.tensor, 512 * h, [[0, 128], [1, 512]]))

                @block.vector
                def _(e, h=h, ldc=ldc):
                    ve.wait_dma(sy, ldc[0])
                    ve.inc(ve.e.tensor_tensor(opn[:, 512 * h:512 * (h + 1)],
                                       opT[:, 512 * h:512 * (h + 1)], lbc[:], ALU.mult))
                    ve.e.drain()

            if True:
                pw = [pstE[:, 512 * i:512 * (i + 1)] for i in range(4)]
                for eg in range(4):
                    for h in range(NH):
                        slot = h % 2
                        ldw = [0]

                        @block.gpsimd
                        def _(e, eg=eg, h=h, slot=slot, ldw=ldw):
                            if eg * NH + h >= 2:
                                gp.wait(te)
                            ldw[0] = slot_dma(gp, woS[slot], wot[slot][:],
                                              wo_in[128 * h:128 * (h + 1), 512 * eg:512 * (eg + 1)])

                        @block.tensor
                        def _(e, eg=eg, h=h, slot=slot, ldw=ldw):
                            te.e.wait_ge(woS[slot], ldw[0])
                            if h == 0:
                                te.wait(ve)
                            for qs_ in range(4):
                                mm = te.e.matmul(pw[qs_],
                                                 opn[:, 512 * h + 128 * qs_:512 * h + 128 * (qs_ + 1)],
                                                 wot[slot][:], start=(h == 0), stop=(h == NH - 1))
                                if qs_ == 3:
                                    te.inc(mm)

                    for qs_ in range(4):
                        @block.vector
                        def _(e, eg=eg, qs_=qs_):
                            ve.wait(te)
                            ve.wait_dma(gp)
                            ve.inc(ve.e.tensor_copy(wsc2[:], pw[qs_]))
                            ve.e.drain()

                        @block.gpsimd
                        def _(e, eg=eg, qs_=qs_, qt=qt):
                            gp.wait(ve)
                            row = (512 if qt == 0 else 0) + 128 * qs_
                            gp.dma(acc_d[row:row + 128, 512 * eg:512 * (eg + 1)],
                                   wsc2[:], accum_op=ALU.add)


if __name__ == "__main__":
    pass


# version 03
# speedup vs baseline: 1.0536x; 1.0048x over previous
import math
import os
import numpy as np
import ml_dtypes
import concourse.bass as bass
import concourse.mybir as mybir
from concourse.bass_utils import run_bass_kernel_spmd
from concourse.replica_groups import maybe_share_collective_output_space
from contextlib import ExitStack

F32 = mybir.dt.float32
I32 = mybir.dt.int32
BF16 = mybir.dt.bfloat16
AF = mybir.ActivationFunctionType
ALU = mybir.AluOpType
AX = mybir.AxisListType
bfdt = ml_dtypes.bfloat16

NCORE = 8
S, E, NH, HD, II = 8192, 2048, 16, 128, 8192
T = 1024
NT = 8
RG = [list(range(NCORE))]
CSHIFT = 14.0
KILL = -30000.0
C2048 = float(1.0 / math.sqrt(2048.0))
C8192 = float(1.0 / math.sqrt(8192.0))
WCNT = float(8192 * 2048)

DEBUG_TAPS = False
TAP_LIST = ()
PH = 9
P3S = 9
KOG = 12
KEVAC = 1
KF = int(os.environ.get("KF", "16"))
KNWR = int(os.environ.get("KNWR", "4"))
PH = 9
P3S = 9
KOG = 12
KEVAC = 1
KF = int(os.environ.get("KF", "16"))
KNWR = int(os.environ.get("KNWR", "4"))


def hadamard(n):
    h = np.array([[1.0]], dtype=np.float64)
    while h.shape[0] < n:
        h = np.block([[h, h], [h, -h]])
    return h


class Eng:
    def __init__(self, e, sem, dma_sem=None):
        self.e = e
        self.sem = sem
        self.dsem = dma_sem
        self.n = 0
        self.dn = 0

    def dma(self, out, in_, **kw):
        self.e.dma_start(out=out, in_=in_, **kw).then_inc(self.dsem, 16)
        self.dn += 16
        return self.dn

    def dmaT(self, out, in_):
        self.e.dma_start_transpose(out, in_).then_inc(self.dsem, 16)
        self.dn += 16
        return self.dn

    def inc(self, instr):
        instr.then_inc(self.sem, 1)
        self.n += 1
        return self.n

    def wait(self, other, val=None):
        v = other.n if val is None else val
        if v > 0:
            self.e.wait_ge(other.sem, v)

    def wait_dma(self, other, val=None):
        v = other.dn if val is None else val
        if v > 0:
            self.e.wait_ge(other.dsem, v)


def make_inmaps(L0_attn_x, L0_attn_Wqkv, L0_attn_Wo, L0_ffn_w_gate, L0_ffn_w_up, L0_ffn_w_down):
    x = np.ascontiguousarray(np.asarray(L0_attn_x, dtype=np.float32).reshape(S, E))
    wqkv = np.ascontiguousarray(np.asarray(L0_attn_Wqkv, dtype=np.float32))
    wo = np.ascontiguousarray(np.asarray(L0_attn_Wo, dtype=np.float32))
    wg = np.asarray(L0_ffn_w_gate, dtype=np.float32)
    wu = np.asarray(L0_ffn_w_up, dtype=np.float32)
    wd = np.asarray(L0_ffn_w_down, dtype=np.float32)

    h128 = hadamard(128).astype(bfdt)
    i = np.arange(HD // 2, dtype=np.float32)
    inv_freq = (1.0 / 10000.0 ** (2.0 * i / HD)).astype(np.float32)
    pos = np.arange(S, dtype=np.float32)
    th = np.outer(pos, inv_freq).astype(np.float32)
    cosf = np.cos(th).astype(np.float32)
    sinf = np.sin(th).astype(np.float32)
    qs = np.float32(1.0 / math.sqrt(HD))
    kk = np.arange(512)
    dmask = np.where(kk[:, None] <= kk[None, :], 0.0, KILL).astype(bfdt)

    in_maps = []
    for c in range(NCORE):
        rows_b = slice(512 * c, 512 * c + 512)
        rows_a = slice(8192 - 512 * (c + 1), 8192 - 512 * c)
        own = np.concatenate([x[rows_b], x[rows_a]], axis=0)
        cos_own = np.concatenate([cosf[rows_b], cosf[rows_a]], axis=0)
        sin_own = np.concatenate([sinf[rows_b], sinf[rows_a]], axis=0)
        kb = np.full((32,), -CSHIFT, np.float32)
        for t_ in range(1, 16):
            if t_ - 1 >= 15 - c:
                kb[t_] = KILL
        for t_ in range(1, 8):
            if t_ - 1 >= c:
                kb[16 + t_] = KILL
        kb[24:] = KILL
        in_maps.append({
            "x_own": own,
            "wqkv": wqkv,
            "wo": wo,
            "wg_sh": np.ascontiguousarray(wg[1024 * c:1024 * (c + 1)]),
            "wu_sh": np.ascontiguousarray(wu[1024 * c:1024 * (c + 1)]),
            "wd_sh": np.ascontiguousarray(wd[:, 1024 * c:1024 * (c + 1)]),
            "h128": h128,
            "cosq": (cos_own * qs).astype(np.float32),
            "sinq": (sin_own * qs).astype(np.float32),
            "cosk": cos_own,
            "sink": sin_own,
            "killb": np.broadcast_to(kb, (128, 32)).copy(),
            "dmask": dmask,
        })

    return in_maps


def kernel(**inputs):
    in_maps = make_inmaps(**inputs)
    nc = build_nc()
    ncr = NCORE
    res = run_bass_kernel_spmd(nc, in_maps, core_ids=list(range(NCORE)))

    out = np.zeros((S, E), np.float32)
    for c in range(ncr):
        r = res.results[c]["out_own"]
        out[512 * c:512 * c + 512] = r[0:512]
        out[8192 - 512 * (c + 1):8192 - 512 * c] = r[512:1024]
    kernel.taps = res.results
    return out.reshape(1, S, E)


def build_nc():
    nc = bass.Bass("TRN2", target_bir_lowering=False)

    x_in = nc.dram_tensor("x_own", [T, E], F32, kind="ExternalInput").ap()
    wqkv_in = nc.dram_tensor("wqkv", [E, 3 * E], F32, kind="ExternalInput").ap()
    wo_in = nc.dram_tensor("wo", [E, E], F32, kind="ExternalInput").ap()
    wg_in = nc.dram_tensor("wg_sh", [1024, E], F32, kind="ExternalInput").ap()
    wu_in = nc.dram_tensor("wu_sh", [1024, E], F32, kind="ExternalInput").ap()
    wd_in = nc.dram_tensor("wd_sh", [E, 1024], F32, kind="ExternalInput").ap()
    h128_in = nc.dram_tensor("h128", [128, 128], BF16, kind="ExternalInput").ap()
    cosq_in = nc.dram_tensor("cosq", [T, 64], F32, kind="ExternalInput").ap()
    sinq_in = nc.dram_tensor("sinq", [T, 64], F32, kind="ExternalInput").ap()
    cosk_in = nc.dram_tensor("cosk", [T, 64], F32, kind="ExternalInput").ap()
    sink_in = nc.dram_tensor("sink", [T, 64], F32, kind="ExternalInput").ap()
    killb_in = nc.dram_tensor("killb", [128, 32], F32, kind="ExternalInput").ap()
    dmask_in = nc.dram_tensor("dmask", [512, 512], BF16, kind="ExternalInput").ap()
    out_d = nc.dram_tensor("out_own", [T, E], F32, kind="ExternalOutput").ap()
    tp_xchS = nc.dram_tensor("tp_xchS", [T, E], BF16, kind="ExternalOutput").ap() if DEBUG_TAPS else None

    taps = {}

    def tap(name, shape, dtype=F32):
        if DEBUG_TAPS and name in TAP_LIST:
            taps[name] = nc.dram_tensor(name, shape, dtype, kind="ExternalOutput").ap()
            return taps[name]
        return None

    xchi_d = nc.dram_tensor("xchi_d", [T, E], BF16).ap()
    xclo_d = nc.dram_tensor("xclo_d", [T, E], BF16).ap()
    q_tm = nc.dram_tensor("q_tm", [T, E], BF16).ap()
    k_tm = nc.dram_tensor("k_tm", [T, E], BF16).ap()
    kT_own = nc.dram_tensor("kT_own", [E, T], BF16).ap()
    v_own_d = nc.dram_tensor("v_own", [T, E], BF16).ap()
    sp_ag = maybe_share_collective_output_space("AllGather", RG)

    kT_all = nc.dram_tensor("kT_all", [NCORE, E, T], BF16, addr_space=sp_ag).ap()
    v_all = nc.dram_tensor("v_all", [NCORE, T, E], BF16, addr_space=sp_ag).ap()
    wg_tern = nc.dram_tensor("wg_tern", [1024, E], BF16).ap()
    wu_tern = nc.dram_tensor("wu_tern", [1024, E], BF16).ap()
    wd_tern = nc.dram_tensor("wd_tern", [E, 1024], BF16).ap()
    wgT_own = nc.dram_tensor("wgT_own", [E, 1024], BF16).ap()
    wuT_own = nc.dram_tensor("wuT_own", [E, 1024], BF16).ap()
    wdT_own = nc.dram_tensor("wdT_own", [1024, E], BF16).ap()
    wgT_all = nc.dram_tensor("wgT_all", [NCORE, E, 1024], BF16, addr_space=sp_ag).ap()
    wuT_all = nc.dram_tensor("wuT_all", [NCORE, E, 1024], BF16, addr_space=sp_ag).ap()
    wdT_all = nc.dram_tensor("wdT_all", [NCORE, 1024, E], BF16, addr_space=sp_ag).ap()
    sp_ar = maybe_share_collective_output_space("AllReduce", RG)

    ws_in = nc.dram_tensor("ws_in", [1, 8], F32).ap()
    ws_out = nc.dram_tensor("ws_out", [1, 8], F32, addr_space=sp_ar).ap()
    part_d = nc.dram_tensor("part_d", [4, 128], F32).ap()
    scal_d = nc.dram_tensor("scal_d", [1, 64], F32).ap()
    y1hi_d = nc.dram_tensor("y1hi_d", [E, T], BF16).ap()
    y1lo_d = nc.dram_tensor("y1lo_d", [E, T], BF16).ap()
    q1_tm = nc.dram_tensor("q1_tm", [T, E], BF16).ap()
    a_row_d = nc.dram_tensor("a_row_d", [1, T], F32).ap()
    y2hi_d = nc.dram_tensor("y2hi_d", [II, 256], BF16).ap()
    y2lo_d = nc.dram_tensor("y2lo_d", [II, 256], BF16).ap()
    q2_tm = nc.dram_tensor("q2_tm", [256, II], BF16).ap()
    acc_d = nc.dram_tensor("acc_d", [T, E], F32).ap()
    linv_d = nc.dram_tensor("linv_d", [1, NH * 512], F32).ap()

    with ExitStack() as _st:
        sb = lambda *a: _st.enter_context(nc.sbuf_tensor(*a))
        sem = lambda n: _st.enter_context(nc.semaphore(n))
        ones_b = sb("ones_b", [128, 1], BF16)
        h128t = sb("h128t", [128, 128], BF16)
        killbt = sb("killbt", [128, 32], F32)
        dmt = sb("dmt", [128, 2048], BF16)
        cosqt = sb("cosqt", [128, 512], F32)
        sinqt = sb("sinqt", [128, 512], F32)
        coskt = sb("coskt", [128, 512], F32)
        sinkt = sb("sinkt", [128, 512], F32)
        rstd = sb("rstd", [128, 8], F32)
        a1c = sb("a1c", [128, 8], F32)
        dsc = sb("dsc", [128, 8], F32)
        wsc_bc = sb("wsc_bc", [128, 8], F32)
        tiny = sb("tiny", [128, 16], F32)
        sQd = sem("sQ"); gQd = sem("gQ"); aQd = sem("aQ"); vQd = sem("vQ")
        tS = sem("tS"); vS = sem("vS"); sS = sem("sS"); gS = sem("gS"); ccS = sem("ccS")
        wrS = [sem(f"wrS{i}") for i in range(4)]
        guS = [sem(f"guS{i}") for i in range(2)]
        wdS = [sem(f"wdS{i}") for i in range(2)]
        woS = [sem(f"woS{i}") for i in range(2)]
        kvS = [sem(f"kvS{i}") for i in range(2)]
        slotcnt = {}

        def slot_dma(eng, semh, out, in_, **kw):
            eng.e.dma_start(out=out, in_=in_, **kw).then_inc(semh, 16)
            slotcnt[id(semh)] = slotcnt.get(id(semh), 0) + 16
            return slotcnt[id(semh)]
        block = _st.enter_context(nc.Block())

        sy = Eng(nc.sync, None, sQd)
        gp = Eng(nc.gpsimd, gS, gQd)
        sc = Eng(nc.scalar, sS, aQd)
        ve = Eng(nc.vector, vS, vQd)
        te = Eng(nc.tensor, tS, None)
        cc_n = [0]
        ALL = [sy, gp, sc, ve, te]

        def cc(kind, op, ins, outs):
            gp.e.collective_compute(kind, op, replica_groups=RG, ins=ins, outs=outs).then_inc(ccS, 1)
            cc_n[0] += 1
            return cc_n[0]

        def barrier(ccv=None):
            snap_d = [(o, o.dn) for o in (sy, gp, sc, ve)]
            snap_n = [(o, o.n) for o in (gp, sc, ve, te)]
            ccv = cc_n[0] if ccv is None else ccv
            for sect, w in ((block.sync, sy), (block.gpsimd, gp), (block.scalar, sc),
                            (block.vector, ve), (block.tensor, te)):
                def _f(e, w=w):
                    for o, v in snap_d:
                        if v > 0:
                            w.e.wait_ge(o.dsem, v)
                    for o, v in snap_n:
                        if v > 0:
                            w.e.wait_ge(o.sem, v)
                    if ccv > 0:
                        w.e.wait_ge(ccS, ccv)
                sect(_f)

        @block.sync
        def _(e):
            sy.dma(h128t[:], h128_in[:])
            sy.dma(killbt[:], killb_in[:])
            sy.dma(dmt[:], bass.AP(dmask_in.tensor, 0, [[512, 128], [65536, 4], [1, 512]]))
            for j in range(NT):
                sy.dma(cosqt[:, 64 * j:64 * (j + 1)], cosq_in[128 * j:128 * (j + 1), :])
                sy.dma(sinqt[:, 64 * j:64 * (j + 1)], sinq_in[128 * j:128 * (j + 1), :])
                sy.dma(coskt[:, 64 * j:64 * (j + 1)], cosk_in[128 * j:128 * (j + 1), :])
                sy.dma(sinkt[:, 64 * j:64 * (j + 1)], sink_in[128 * j:128 * (j + 1), :])

        @block.vector
        def _(e):
            ve.inc(ve.e.memset(ones_b[:], 1.0))

        with ExitStack() as _s1:
            sb1 = lambda *a: _s1.enter_context(nc.sbuf_tensor(*a))
            xt = sb1("xt", [128, E], F32)
            sqs = sb1("sqs", [128, E], F32)
            xc = sb1("xc", [128, E], F32)
            xch = sb1("xch", [128, E], BF16)
            xcl = sb1("xcl", [128, E], BF16)

            for j in range(NT):
                ld = [0]

                @block.sync
                def _(e, j=j, ld=ld):
                    sy.wait(sc)
                    sy.wait(ve)
                    ld[0] = sy.dma(xt[:], x_in[128 * j:128 * (j + 1), :])

                @block.scalar
                def _(e, j=j, ld=ld):
                    sc.wait_dma(sy, ld[0])
                    sc.inc(sc.e.activation(sqs[:], xt[:], AF.Square, bias=0.0, scale=1.0,
                                           accum_out=tiny[:, 0:1]))

                @block.vector
                def _(e, j=j, ld=ld):
                    ve.wait_dma(sy, ld[0])
                    ve.e.tensor_reduce(tiny[:, 5:6], xt[:], AX.X, ALU.add)
                    ve.e.drain()
                    ve.e.tensor_scalar(tiny[:, 1:2], tiny[:, 5:6], 1.0 / E, None, ALU.mult)
                    ve.wait(sc)
                    ve.e.tensor_scalar(tiny[:, 6:7], tiny[:, 0:1], 1.0 / E, None, ALU.mult)
                    ve.e.drain()
                    ve.e.tensor_tensor(tiny[:, 3:4], tiny[:, 1:2], tiny[:, 1:2], ALU.mult)
                    ve.e.drain()
                    ve.e.tensor_tensor(tiny[:, 7:8], tiny[:, 6:7], tiny[:, 3:4], ALU.subtract)
                    ve.e.drain()
                    ve.e.tensor_scalar(tiny[:, 2:3], tiny[:, 7:8], 1e-5, None, ALU.add)
                    ve.e.drain()
                    ve.inc(ve.e.reciprocal(tiny[:, 4:5], tiny[:, 2:3]))
                    ve.e.drain()

                @block.scalar
                def _(e, j=j):
                    sc.wait(ve)
                    sc.inc(sc.e.activation(tiny[:, 8:9], tiny[:, 4:5], AF.Sqrt, bias=0.0, scale=1.0))

                @block.vector
                def _(e, j=j):
                    ve.wait(sc)
                    for _it in range(2):
                        ve.e.tensor_tensor(tiny[:, 9:10], tiny[:, 8:9], tiny[:, 8:9], ALU.mult)
                        ve.e.drain()
                        ve.e.tensor_tensor(tiny[:, 10:11], tiny[:, 9:10], tiny[:, 2:3], ALU.mult)
                        ve.e.drain()
                        ve.e.tensor_scalar(tiny[:, 11:12], tiny[:, 10:11], -0.5, 1.5, ALU.mult, ALU.add)
                        ve.e.drain()
                        ve.e.tensor_tensor(tiny[:, 8:9], tiny[:, 8:9], tiny[:, 11:12], ALU.mult)
                        ve.e.drain()
                    ve.e.tensor_copy(rstd[:, j:j + 1], tiny[:, 8:9])
                    ve.e.drain()
                    ve.wait_dma(gp)
                    ve.e.tensor_scalar(xc[:], xt[:], tiny[:, 1:2], None, ALU.subtract)
                    ve.e.drain()
                    ve.e.tensor_copy(xch[:], xc[:])
                    ve.e.drain()
                    ve.inc(ve.e.tensor_tensor(xcl[:], xc[:], xch[:], ALU.subtract))

                @block.gpsimd
                def _(e, j=j):
                    gp.wait(ve)
                    gp.dma(xchi_d[128 * j:128 * (j + 1), :], xch[:])
                    gp.dma(xclo_d[128 * j:128 * (j + 1), :], xcl[:])
                    if DEBUG_TAPS:
                        gp.dma(tp_xchS[128 * j:128 * (j + 1), :], xch[:])


            barrier()

        _sxc = ExitStack()
        XCHI_T = _sxc.enter_context(nc.sbuf_tensor("XCHI_T", [128, 16384], BF16))
        XCLO_T = _sxc.enter_context(nc.sbuf_tensor("XCLO_T", [128, 16384], BF16))

        @block.scalar
        def _(e):
            for a in range(16):
                sc.dmaT(XCHI_T[:, 1024 * a:1024 * (a + 1)], xchi_d[:, 128 * a:128 * (a + 1)])
                sc.dmaT(XCLO_T[:, 1024 * a:1024 * (a + 1)], xclo_d[:, 128 * a:128 * (a + 1)])

        if PH >= 2:
            with ExitStack() as _s2:
                sb2 = lambda *a: _s2.enter_context(nc.sbuf_tensor(*a))
                wt = sb2("wt", [128, E], F32)
                wtern = sb2("wtern", [128, E], BF16)
                wqi = sb2("wqi", [128, E], I32)
                prow = sb2("prow", [1, 128], F32)
                srow = sb2("srow", [1, 8], F32)
                wtt = sb2("wtt", [128, E], BF16)

                @block.vector
                def _(e):
                    ve.inc(ve.e.memset(srow[:], 0.0))

                mats = [(wg_in, 8, E, wg_tern), (wu_in, 8, E, wu_tern), (wd_in, 16, 1024, wd_tern)]
                for mi, (w_in, ntile, tw, _w) in enumerate(mats):
                    for t_ in range(ntile):
                        ld = [0]

                        @block.sync
                        def _(e, mi=mi, t_=t_, w_in=w_in, tw=tw, ld=ld):
                            sy.wait(ve)
                            ld[0] = sy.dma(wt[:, 0:tw], w_in[128 * t_:128 * (t_ + 1), :])

                        @block.vector
                        def _(e, mi=mi, t_=t_, tw=tw, ld=ld, first=(t_ == 0)):
                            ve.wait_dma(sy, ld[0])
                            ve.e.tensor_reduce(tiny[:, 0:1], wt[:, 0:tw], AX.X, ALU.add,
                                               apply_absolute_value=True)
                            ve.e.drain()
                            if first:
                                ve.inc(ve.e.tensor_copy(tiny[:, 8 + mi:9 + mi], tiny[:, 0:1]))
                            else:
                                ve.inc(ve.e.tensor_tensor(tiny[:, 8 + mi:9 + mi],
                                                          tiny[:, 8 + mi:9 + mi], tiny[:, 0:1], ALU.add))
                            ve.e.drain()

                @block.gpsimd
                def _(e):
                    gp.wait(ve)
                    for mi in range(3):
                        gp.dma(bass.AP(part_d.tensor, 128 * mi, [[1, 128], [1, 1]]), tiny[:, 8 + mi:9 + mi])

                for mi in range(3):
                    ld = [0]

                    @block.sync
                    def _(e, mi=mi, ld=ld):
                        sy.wait(ve)
                        sy.wait_dma(gp)
                        ld[0] = sy.dma(prow[:, :], bass.AP(part_d.tensor, 128 * mi, [[0, 1], [1, 128]]))

                    @block.vector
                    def _(e, mi=mi, ld=ld):
                        ve.wait_dma(sy, ld[0])
                        ve.inc(ve.e.tensor_reduce(srow[:, mi:mi + 1], prow[:], AX.X, ALU.add))
                        ve.e.drain()

                @block.gpsimd
                def _(e):
                    gp.wait(ve)
                    gp.dma(ws_in[:], srow[:])
                    gp.e.wait_ge(gQd, gp.dn)
                    cc("AllReduce", ALU.add, [ws_in[:]], [ws_out[:]])

                @block.sync
                def _(e):
                    sy.e.wait_ge(ccS, cc_n[0])
                    sy.dma(srow[:], ws_out[:])

                @block.vector
                def _(e):
                    ve.wait_dma(sy)
                    ve.e.tensor_scalar(srow[:, 4:8], srow[:, 0:4], 1.0 / WCNT, None, ALU.mult)
                    ve.e.drain()
                    ve.e.tensor_scalar(srow[:, 4:8], srow[:, 4:8], 1e-5, None, ALU.max)
                    ve.e.drain()
                    ve.inc(ve.e.reciprocal(srow[:, 0:4], srow[:, 4:8]))
                    ve.e.drain()

                @block.gpsimd
                def _(e):
                    gp.wait(ve)
                    gp.dma(bass.AP(scal_d.tensor, 0, [[0, 1], [1, 8]]), srow[:, 0:8])

                @block.sync
                def _(e):
                    sy.wait_dma(gp)
                    sy.dma(wsc_bc[:, 0:8], bass.AP(scal_d.tensor, 0, [[0, 128], [1, 8]]))

                for mi, (w_in, ntile, tw, wtern_d) in enumerate(mats):
                    for t_ in range(ntile):
                        ld = [0]

                        @block.sync
                        def _(e, mi=mi, t_=t_, w_in=w_in, tw=tw, ld=ld):
                            sy.wait(ve)
                            ld[0] = sy.dma(wt[:, 0:tw], w_in[128 * t_:128 * (t_ + 1), :])

                        @block.vector
                        def _(e, mi=mi, t_=t_, tw=tw, ld=ld):
                            ve.wait_dma(sy)
                            ve.wait_dma(gp)
                            ve.e.tensor_scalar(wt[:, 0:tw], wt[:, 0:tw], wsc_bc[:, mi:mi + 1], None, ALU.mult)
                            ve.e.drain()
                            ve.e.tensor_copy(wqi[:, 0:tw], wt[:, 0:tw])
                            ve.e.drain()
                            ve.inc(ve.e.tensor_scalar(wtern[:, 0:tw], wqi[:, 0:tw], 1.0, -1.0, ALU.min, ALU.max))

                        @block.gpsimd
                        def _(e, mi=mi, t_=t_, tw=tw, wtern_d=wtern_d):
                            gp.wait(ve)
                            gp.dma(wtern_d[128 * t_:128 * (t_ + 1), 0:tw], wtern[:, 0:tw])

                def transpose_out(src_d, dst_d, ncolblk, srows):
                    for a in range(ncolblk):
                        ldc = [0]

                        @block.scalar
                        def _(e, a=a, ldc=ldc):
                            sc.wait_dma(gp)
                            ldc[0] = sc.dmaT(wtt[:, 0:srows], src_d[:, 128 * a:128 * (a + 1)])

                        @block.gpsimd
                        def _(e, a=a, ldc=ldc):
                            gp.wait_dma(sc, ldc[0])
                            gp.dma(dst_d[128 * a:128 * (a + 1), 0:srows], wtt[:, 0:srows])

                transpose_out(wg_tern, wgT_own, 16, 1024)
                transpose_out(wu_tern, wuT_own, 16, 1024)
                transpose_out(wd_tern, wdT_own, 8, 2048)

                @block.gpsimd
                def _(e):
                    gp.e.wait_ge(gQd, gp.dn)
                    cc("AllGather", ALU.bypass, [wgT_own[:]], [wgT_all[:]])
                    cc("AllGather", ALU.bypass, [wuT_own[:]], [wuT_all[:]])
                    cc("AllGather", ALU.bypass, [wdT_own[:]], [wdT_all[:]])

                barrier(ccv=0)
        AGW_CC = cc_n[0]

        if PH >= 3:
            with ExitStack() as _s3:
                sb3 = lambda *a: _s3.enter_context(nc.sbuf_tensor(*a))
                NWR = 4
                wring = [sb3(f"wr{i}", [128, 512], BF16) for i in range(NWR)]
                crq = sb3("crq", [128, 2048], F32)
                srq = sb3("srq", [128, 2048], F32)
                crk = sb3("crk", [128, 2048], F32)
                srk = sb3("srk", [128, 2048], F32)
                rs1 = sb3("rs1", [128, 256], F32)
                rs2 = sb3("rs2", [128, 256], F32)
                rs3 = sb3("rs3", [128, 256], F32)
                rs4 = sb3("rs4", [128, 256], F32)
                ro = sb3("ro", [128, 4096], F32)
                ktt2 = sb3("ktt2", [128, 1024], BF16)
                ps8 = [_s3.enter_context(nc.psum_tensor(f"pq{i}", [128, 512], F32)) for i in range(8)]

                @block.vector
                def _(e):
                    for j in range(NT):
                        for r in range(4):
                            cs_ = slice(256 * j + 64 * r, 256 * j + 64 * (r + 1))
                            js = slice(64 * j, 64 * (j + 1))
                            ve.e.tensor_scalar(crq[:, cs_], cosqt[:, js], rstd[:, j:j + 1], None, ALU.mult)
                            ve.e.tensor_scalar(srq[:, cs_], sinqt[:, js], rstd[:, j:j + 1], None, ALU.mult)
                            ve.e.tensor_scalar(crk[:, cs_], coskt[:, js], rstd[:, j:j + 1], None, ALU.mult)
                            ve.inc(ve.e.tensor_scalar(srk[:, cs_], sinkt[:, js], rstd[:, j:j + 1], None, ALU.mult))
                    ve.e.drain()

                qk_ck = []
                for og in range((KOG if P3S >= 2 else 0)):
                    for grp in range(2):
                        gi = og * 2 + grp
                        jlist = [4 * grp + jj for jj in range(4)]

                        te_base = te.n
                        gp_base = gp.dn

                        wr_tgt = []

                        @block.gpsimd
                        def _(e, og=og, grp=grp, te_base=te_base, wr_tgt=wr_tgt):
                            for f in range(16):
                                need = te_base - (NWR - 1) + f
                                if need > 0:
                                    gp.e.wait_ge(te.sem, need)
                                wr_tgt.append(slot_dma(gp, wrS[f % NWR], wring[f % NWR][:],
                                                       wqkv_in[128 * f:128 * (f + 1), 512 * og:512 * (og + 1)]))

                        @block.tensor
                        def _(e, og=og, grp=grp, gi=gi, jlist=jlist, gp_base=gp_base, wr_tgt=wr_tgt):
                            if gi >= 2:
                                te.wait(ve, qk_ck[gi - 2])
                            if gi == 0:
                                te.wait(ve)
                                te.wait_dma(sc)
                            for f in range(16):
                                te.e.wait_ge(wrS[f % NWR], wr_tgt[f])
                                for j in jlist:
                                    mm = te.e.matmul(ps8[j][:],
                                                     XCHI_T[:, 1024 * f + 128 * j:1024 * f + 128 * (j + 1)],
                                                     wring[f % NWR][:], start=(f == 0), stop=(f == 15))
                                    if j == jlist[-1]:
                                        te.inc(mm)

                        if KEVAC == 0:
                            @block.vector
                            def _(e):
                                ve.wait(te)
                                for j in jlist:
                                    cp = ve.e.tensor_copy(ro[:, 512 * j:512 * (j + 1)], ps8[j][:])
                                    if j == jlist[-1]:
                                        ve.inc(cp)
                                ve.e.drain()
                            qk_ck.append(ve.n)
                        elif og < 8:
                            ctab, stab = (crq, srq) if og < 4 else (crk, srk)
                            dst = q_tm if og < 4 else k_tm
                            col = 512 * (og % 4)

                            @block.vector
                            def _(e, og=og, jlist=jlist, ctab=ctab, stab=stab):
                                ve.wait(te)
                                ve.wait_dma(gp)
                                for j in jlist:
                                    ps = ps8[j]
                                    cj = slice(256 * j, 256 * (j + 1))
                                    ev = bass.AP(ps, 0, [[512, 128], [2, 256]])
                                    od = bass.AP(ps, 1, [[512, 128], [2, 256]])
                                    ve.e.tensor_tensor(rs1[:], ev, ctab[:, cj], ALU.mult)
                                    ve.e.tensor_tensor(rs2[:], od, stab[:, cj], ALU.mult)
                                    ve.e.tensor_tensor(rs3[:], ev, stab[:, cj], ALU.mult)
                                    ve.e.tensor_tensor(rs4[:], od, ctab[:, cj], ALU.mult)
                                    ve.e.drain()
                                    roe = bass.AP(ro, 512 * j, [[4096, 128], [2, 256]])
                                    roo = bass.AP(ro, 512 * j + 1, [[4096, 128], [2, 256]])
                                    ve.e.tensor_tensor(roe, rs1[:], rs2[:], ALU.subtract)
                                    ve.inc(ve.e.tensor_tensor(roo, rs3[:], rs4[:], ALU.add))
                                    ve.e.drain()
                            qk_ck.append(ve.n)

                            @block.gpsimd
                            def _(e, jlist=jlist, dst=dst, col=col):
                                gp.wait(ve)
                                for j in jlist:
                                    gp.dma(dst[128 * j:128 * (j + 1), col:col + 512],
                                           ro[:, 512 * j:512 * (j + 1)])
                        else:
                            col = 512 * (og - 8)

                            @block.vector
                            def _(e, jlist=jlist):
                                ve.wait(te)
                                ve.wait_dma(gp)
                                for j in jlist:
                                    op = ve.e.tensor_scalar(ro[:, 512 * j:512 * (j + 1)], ps8[j][:],
                                                            rstd[:, j:j + 1], None, ALU.mult)
                                    if j == jlist[-1]:
                                        ve.inc(op)
                                ve.e.drain()
                            qk_ck.append(ve.n)

                            @block.gpsimd
                            def _(e, jlist=jlist, col=col):
                                gp.wait(ve)
                                for j in jlist:
                                    gp.dma(v_own_d[128 * j:128 * (j + 1), col:col + 512],
                                           ro[:, 512 * j:512 * (j + 1)])

                for a in range(16 if P3S >= 3 else 0):
                    ldc = [0]

                    @block.scalar
                    def _(e, a=a, ldc=ldc):
                        sc.wait_dma(gp)
                        ldc[0] = sc.dmaT(ktt2[:], k_tm[:, 128 * a:128 * (a + 1)])

                    @block.gpsimd
                    def _(e, a=a, ldc=ldc):
                        gp.wait_dma(sc, ldc[0])
                        gp.dma(kT_own[128 * a:128 * (a + 1), :], ktt2[:])

                @block.gpsimd
                def _(e):
                    gp.e.wait_ge(gQd, gp.dn)
                    if P3S >= 4:
                        cc("AllGather", ALU.bypass, [kT_own[:]], [kT_all[:]])
                        cc("AllGather", ALU.bypass, [v_own_d[:]], [v_all[:]])

                barrier(ccv=AGW_CC)

        if PH >= 4:
            with ExitStack() as _s4:
                sb4 = lambda *a: _s4.enter_context(nc.sbuf_tensor(*a))
                F1 = sb4("F1", [128, 16384], F32)
                fh = sb4("fh", [128, 1024], BF16)
                fl = sb4("fl", [128, 1024], BF16)
                th_t = sb4("th_t", [128, 2048], BF16)
                tl_t = sb4("tl_t", [128, 2048], BF16)
                fscr = sb4("fscr", [128, 2048], F32)
                mscr = sb4("mscr", [128, 2048], F32)
                q1i = sb4("q1i", [128, 2048], I32)
                q1b = sb4("q1b", [128, 2048], BF16)
                pf = [_s4.enter_context(nc.psum_tensor(f"pf{i}", [128, 512], F32)) for i in range(2)]

                for a in range(16):
                    for half in range(2):
                        colsT = slice(1024 * a + 512 * half, 1024 * a + 512 * (half + 1))
                        idx = (a * 2 + half) % 2

                        @block.tensor
                        def _(e, colsT=colsT, idx=idx, first=(a == 0 and half == 0)):
                            te.wait(ve)
                            if first:
                                te.wait_dma(sc)
                            te.e.matmul(pf[idx][:], h128t[:], XCHI_T[:, colsT], start=True, stop=False)
                            te.inc(te.e.matmul(pf[idx][:], h128t[:], XCLO_T[:, colsT], start=False, stop=True))

                        @block.vector
                        def _(e, colsT=colsT, idx=idx):
                            ve.wait(te)
                            ve.inc(ve.e.tensor_copy(F1[:, colsT], pf[idx][:]))

                @block.vector
                def _(e):
                    ve.wait(te)
                    for st_ in range(4):
                        bs = 1 << st_
                        pairs = [(a, a + bs) for a in range(16) if (a // bs) % 2 == 0]
                        for (pa, pb) in pairs:
                            ve.e.tensor_tensor(F1[:, 1024 * pa:1024 * (pa + 1)],
                                               F1[:, 1024 * pa:1024 * (pa + 1)],
                                               F1[:, 1024 * pb:1024 * (pb + 1)], ALU.add)
                        ve.e.drain()
                        for pi, (pa, pb) in enumerate(pairs):
                            op = ve.e.scalar_tensor_tensor(F1[:, 1024 * pb:1024 * (pb + 1)],
                                                           F1[:, 1024 * pb:1024 * (pb + 1)], -2.0,
                                                           F1[:, 1024 * pa:1024 * (pa + 1)], ALU.mult, ALU.add)
                            if st_ == 3 and pi == len(pairs) - 1:
                                ve.inc(op)
                        ve.e.drain()

                for a in range(16):
                    @block.vector
                    def _(e, a=a):
                        ve.wait_dma(gp)
                        cols = slice(1024 * a, 1024 * (a + 1))
                        ve.e.tensor_copy(fh[:], F1[:, cols])
                        ve.e.drain()
                        ve.inc(ve.e.tensor_tensor(fl[:], F1[:, cols], fh[:], ALU.subtract))

                    @block.gpsimd
                    def _(e, a=a):
                        gp.wait(ve)
                        gp.dma(y1hi_d[128 * a:128 * (a + 1), :], fh[:])
                        gp.dma(y1lo_d[128 * a:128 * (a + 1), :], fl[:])

                for j in range(NT):
                    ldc = [0]

                    @block.scalar
                    def _(e, j=j, ldc=ldc):
                        sc.wait_dma(gp)
                        sc.wait(ve)
                        sc.dmaT(th_t[:], y1hi_d[:, 128 * j:128 * (j + 1)])
                        ldc[0] = sc.dmaT(tl_t[:], y1lo_d[:, 128 * j:128 * (j + 1)])

                    @block.vector
                    def _(e, j=j, ldc=ldc):
                        ve.wait_dma(sc, ldc[0])
                        ve.wait_dma(gp)
                        ve.e.tensor_tensor(fscr[:], th_t[:], tl_t[:], ALU.add)
                        ve.e.drain()
                        ve.e.tensor_reduce(tiny[:, 0:1], fscr[:], AX.X, ALU.max, apply_absolute_value=True)
                        ve.e.drain()
                        ve.e.tensor_scalar(tiny[:, 1:2], rstd[:, j:j + 1], C2048, None, ALU.mult)
                        ve.e.drain()
                        ve.e.tensor_tensor(tiny[:, 2:3], tiny[:, 1:2], tiny[:, 0:1], ALU.mult)
                        ve.e.drain()
                        ve.e.tensor_scalar(a1c[:, j:j + 1], tiny[:, 2:3], 1e-5, None, ALU.max)
                        ve.e.drain()
                        ve.e.reciprocal(tiny[:, 3:4], a1c[:, j:j + 1])
                        ve.e.drain()
                        ve.e.tensor_tensor(tiny[:, 4:5], tiny[:, 1:2], tiny[:, 3:4], ALU.mult)
                        ve.e.drain()
                        ve.inc(ve.e.tensor_scalar(tiny[:, 5:6], tiny[:, 4:5], 127.0, None, ALU.mult))
                        ve.e.drain()

                    @block.vector
                    def _(e, j=j):
                        ve.e.drain()
                        ve.e.tensor_scalar(mscr[:], fscr[:], tiny[:, 5:6], None, ALU.mult)
                        ve.e.drain()
                        ve.e.tensor_copy(q1i[:], mscr[:])
                        ve.e.drain()
                        ve.inc(ve.e.tensor_scalar(q1b[:], q1i[:], 127.0, -128.0, ALU.min, ALU.max))

                    @block.gpsimd
                    def _(e, j=j):
                        gp.wait(ve)
                        gp.dma(q1_tm[128 * j:128 * (j + 1), :], q1b[:])

                barrier()
        _sxc.close()

        with ExitStack() as _sff:
            q1T = _sff.enter_context(nc.sbuf_tensor("q1T", [128, 16384], BF16))

            @block.scalar
            def _(e):
                if PH >= 5:
                    for a in range(16):
                        sc.dmaT(q1T[:, 1024 * a:1024 * (a + 1)], q1_tm[:, 128 * a:128 * (a + 1)])

            @block.vector
            def _(e):
                if PH < 5:
                    return
                for j in range(NT):
                    ve.e.tensor_scalar(tiny[:, 0:1], a1c[:, j:j + 1], 1.0 / 127.0, None, ALU.mult)
                    ve.e.drain()
                    ve.inc(ve.e.tensor_tensor(tiny[:, 8 + j:9 + j], tiny[:, 0:1], wsc_bc[:, 4:5], ALU.mult))
                    ve.e.drain()

            @block.gpsimd
            def _(e):
                if PH < 5:
                    return
                gp.wait(ve)
                for j in range(NT):
                    gp.dma(bass.AP(a_row_d.tensor, 128 * j, [[1, 128], [1, 1]]), tiny[:, 8 + j:9 + j])

            for ch in range(4 if PH >= 5 else 0):
                ffn_chunk(nc, block, sy, gp, sc, ve, te, ch,
                          q1T, h128t, wsc_bc, a1c, dsc, tiny,
                          wgT_all, wuT_all, wdT_all, a_row_d, y2hi_d, y2lo_d, q2_tm, acc_d,
                          slot_dma, guS, wdS)

            barrier()

        if PH >= 6:
            attn_phase(nc, block, sy, gp, sc, ve, te, ccS, cc_n,
                       killbt, dmt, ones_b,
                       q_tm, kT_own, v_own_d, kT_all, v_all, wo_in, linv_d, acc_d,
                       slot_dma, woS, kvS)

        if DEBUG_TAPS:
            tp = {}
            for nm, shape, dt_ in [("tp_xchi", [T, E], BF16), ("tp_q1", [T, E], BF16),
                                   ("tp_qtm", [T, E], BF16), ("tp_ktm", [T, E], BF16),
                                   ("tp_vown", [T, E], BF16), ("tp_tern", [1024, E], BF16),
                                   ("tp_acc", [T, E], F32), ("tp_rstd", [128, 8], F32),
                                   ("tp_wsc", [128, 8], F32), ("tp_lin", [1, NH * 512], F32),
                                   ("tp_a1c", [128, 8], F32), ("tp_dsc", [128, 8], F32)]:
                tp[nm] = nc.dram_tensor(nm, shape, dt_, kind="ExternalOutput").ap()

            barrier()

            @block.gpsimd
            def _(e):
                gp.dma(tp["tp_xchi"][:], xchi_d[:])
                gp.dma(tp["tp_q1"][:], q1_tm[:])
                gp.dma(tp["tp_qtm"][:], q_tm[:])
                gp.dma(tp["tp_ktm"][:], k_tm[:])
                gp.dma(tp["tp_vown"][:], v_own_d[:])
                gp.dma(tp["tp_tern"][:], wg_tern[:])
                gp.dma(tp["tp_acc"][:], acc_d[:])
                gp.dma(tp["tp_lin"][:], linv_d[:])

            @block.vector
            def _(e):
                ve.inc(ve.e.tensor_copy(tiny[:, 0:8], rstd[:]))

            @block.gpsimd
            def _(e):
                gp.wait(ve)
                gp.dma(tp["tp_rstd"][:], tiny[:, 0:8])
                gp.dma(tp["tp_wsc"][:], wsc_bc[:])
                gp.dma(tp["tp_a1c"][:], a1c[:])
                gp.dma(tp["tp_dsc"][:], dsc[:])

        with ExitStack() as _s7:
            sb7 = lambda *a: _s7.enter_context(nc.sbuf_tensor(*a))
            xa = sb7("xa", [128, E], F32)
            ac = sb7("ac", [128, E], F32)
            oo = sb7("oo", [128, E], F32)
            barrier()
            for j in range(NT):
                ld = [0]

                @block.sync
                def _(e, j=j, ld=ld):
                    sy.wait(ve)
                    sy.wait_dma(gp)
                    sy.dma(xa[:], x_in[128 * j:128 * (j + 1), :])
                    ld[0] = sy.dma(ac[:], acc_d[128 * j:128 * (j + 1), :])

                @block.vector
                def _(e, j=j, ld=ld):
                    ve.wait_dma(sy, ld[0])
                    ve.inc(ve.e.tensor_tensor(oo[:], ac[:], xa[:], ALU.add))

                @block.gpsimd
                def _(e, j=j):
                    gp.wait(ve)
                    gp.dma(out_d[128 * j:128 * (j + 1), :], oo[:])

            @block.gpsimd
            def _(e):
                gp.e.wait_ge(gQd, gp.dn)

    return nc


def barrier_nocc(ALL, sy, gp, sc, ve, te):
    for w in ALL:
        for o in (sy, gp, sc, ve):
            if o.dn > 0:
                w.e.wait_ge(o.dsem, o.dn)
        for o in (gp, sc, ve, te):
            if o.n > 0:
                w.e.wait_ge(o.sem, o.n)


def ffn_chunk(nc, block, sy, gp, sc, ve, te, ch,
              q1T, h128t, wsc_bc, a1c, dsc, tiny,
              wgT_all, wuT_all, wdT_all, a_row_d, y2hi_d, y2lo_d, q2_tm, acc_d,
              slot_dma, guS, wdS):
    F32_ = F32
    with ExitStack() as _sy1:
        sbc = lambda *a: _sy1.enter_context(nc.sbuf_tensor(*a))
        Y1 = sbc(f"Y1_{ch}", [128, 16384], F32)
        wgt = [sbc(f"wgt{ch}_{i}", [128, 2048], BF16) for i in range(2)]
        wut = [sbc(f"wut{ch}_{i}", [128, 2048], BF16) for i in range(2)]
        abc = sbc(f"abc_{ch}", [128, 256], F32)
        e1s = sbc(f"e1s_{ch}", [128, 256], F32)
        e2s = sbc(f"e2s_{ch}", [128, 256], F32)
        zh = sbc(f"zh_{ch}", [128, 256], BF16)
        zl = sbc(f"zl_{ch}", [128, 256], BF16)
        pg = [_sy1.enter_context(nc.psum_tensor(f"pg{ch}_{i}", [128, 256], F32)) for i in range(2)]
        pu = [_sy1.enter_context(nc.psum_tensor(f"pu{ch}_{i}", [128, 256], F32)) for i in range(2)]
        py = [_sy1.enter_context(nc.psum_tensor(f"py{ch}_{i}", [128, 256], F32)) for i in range(2)]

        @block.sync
        def _(e, ch=ch):
            sy.wait(ve)
            sy.wait_dma(gp)
            sy.dma(abc[:], bass.AP(a_row_d.tensor, 256 * ch, [[0, 128], [1, 256]]))

        vz_ck = []
        y1_ck = []
        for A in range(64):
            slot = A % 2
            slab, icol = A // 8, 128 * (A % 8)
            ldw = [0]

            @block.sync
            def _(e, A=A, slot=slot, slab=slab, icol=icol, ldw=ldw):
                if A >= 2:
                    sy.wait(te)
                elif A == 0:
                    sy.wait(te)
                    sy.wait(ve)
                off_g = slab * (2048 * 1024) + icol
                slot_dma(sy, guS[slot], wgt[slot][:], bass.AP(wgT_all.tensor, off_g,
                                                              [[1024, 128], [131072, 16], [1, 128]]))
                ldw[0] = slot_dma(sy, guS[slot], wut[slot][:], bass.AP(wuT_all.tensor, off_g,
                                                                      [[1024, 128], [131072, 16], [1, 128]]))

            @block.tensor
            def _(e, A=A, slot=slot, ch=ch, ldw=ldw):
                te.e.wait_ge(guS[slot], ldw[0])
                if A == 0:
                    te.wait_dma(sc)
                if A >= 2:
                    te.wait(ve, vz_ck[A - 2])
                    te.wait(ve, y1_ck[A - 2])
                for fc in range(16):
                    rhs = q1T[:, 1024 * fc + 256 * ch:1024 * fc + 256 * ch + 256]
                    te.e.matmul(pg[slot][:], wgt[slot][:, 128 * fc:128 * (fc + 1)], rhs,
                                start=(fc == 0), stop=(fc == 15))
                    mm = te.e.matmul(pu[slot][:], wut[slot][:, 128 * fc:128 * (fc + 1)], rhs,
                                     start=(fc == 0), stop=(fc == 15))
                    if fc == 15:
                        te.inc(mm)

            @block.vector
            def _(e, A=A, slot=slot):
                ve.wait(te)
                if A == 0:
                    ve.wait_dma(sy)
                ve.inc(ve.e.tensor_tensor(e1s[:], pg[slot][:], abc[:], ALU.mult))
                ve.e.drain()

            @block.scalar
            def _(e, A=A):
                sc.wait(ve)
                sc.inc(sc.e.activation(e2s[:], e1s[:], AF.Silu, bias=0.0, scale=1.0))

            @block.vector
            def _(e, A=A, slot=slot):
                ve.wait(sc)
                ve.e.tensor_tensor(e1s[:], e2s[:], pu[slot][:], ALU.mult)
                ve.e.drain()
                ve.e.tensor_copy(zh[:], e1s[:])
                ve.e.drain()
                ve.inc(ve.e.tensor_tensor(zl[:], e1s[:], zh[:], ALU.subtract))
            vz_ck.append(ve.n)

            @block.tensor
            def _(e, A=A, slot=slot):
                te.wait(ve)
                te.e.matmul(py[slot][:], h128t[:], zh[:], start=True, stop=False)
                te.inc(te.e.matmul(py[slot][:], h128t[:], zl[:], start=False, stop=True))

            @block.vector
            def _(e, A=A, slot=slot):
                ve.wait(te)
                ve.inc(ve.e.tensor_copy(Y1[:, 256 * A:256 * (A + 1)], py[slot][:]))
            y1_ck.append(ve.n)

        @block.vector
        def _(e):
            ve.wait(sc)
            ve.wait(te)
            for st_ in range(6):
                bs = 1 << st_
                pairs = [(a, a + bs) for a in range(64) if (a // bs) % 2 == 0]
                for (pa, pb) in pairs:
                    ve.e.tensor_tensor(Y1[:, 256 * pa:256 * (pa + 1)],
                                       Y1[:, 256 * pa:256 * (pa + 1)],
                                       Y1[:, 256 * pb:256 * (pb + 1)], ALU.add)
                ve.e.drain()
                for pi, (pa, pb) in enumerate(pairs):
                    op = ve.e.scalar_tensor_tensor(Y1[:, 256 * pb:256 * (pb + 1)],
                                                   Y1[:, 256 * pb:256 * (pb + 1)], -2.0,
                                                   Y1[:, 256 * pa:256 * (pa + 1)], ALU.mult, ALU.add)
                    if st_ == 5 and pi == len(pairs) - 1:
                        ve.inc(op)
                ve.e.drain()

        for A in range(64):
            @block.vector
            def _(e, A=A):
                ve.wait_dma(gp)
                ve.e.tensor_copy(zh[:], Y1[:, 256 * A:256 * (A + 1)])
                ve.e.drain()
                ve.inc(ve.e.tensor_tensor(zl[:], Y1[:, 256 * A:256 * (A + 1)], zh[:], ALU.subtract))

            @block.gpsimd
            def _(e, A=A):
                gp.wait(ve)
                gp.dma(y2hi_d[128 * A:128 * (A + 1), :], zh[:])
                gp.dma(y2lo_d[128 * A:128 * (A + 1), :], zl[:])

    with ExitStack() as _sy2:
        sbq = lambda *a: _sy2.enter_context(nc.sbuf_tensor(*a))
        qh2 = sbq(f"qh2_{ch}", [128, II], BF16)
        ql2 = sbq(f"ql2_{ch}", [128, II], BF16)
        fs2 = sbq(f"fs2_{ch}", [128, II], F32)
        ms2 = sbq(f"ms2_{ch}", [128, II], F32)
        q2i = sbq(f"q2i_{ch}", [128, II], I32)
        q2b = sbq(f"q2b_{ch}", [128, II], BF16)

        for jj in range(2):
            jt = 2 * ch + jj
            ldc = [0]

            @block.scalar
            def _(e, jj=jj, ldc=ldc):
                sc.wait_dma(gp)
                sc.wait(ve)
                sc.dmaT(qh2[:], y2hi_d[:, 128 * jj:128 * (jj + 1)])
                ldc[0] = sc.dmaT(ql2[:], y2lo_d[:, 128 * jj:128 * (jj + 1)])

            @block.vector
            def _(e, jj=jj, jt=jt, ldc=ldc):
                ve.wait_dma(sc, ldc[0])
                ve.wait_dma(gp)
                ve.e.tensor_tensor(fs2[:], qh2[:], ql2[:], ALU.add)
                ve.e.drain()
                ve.e.tensor_reduce(tiny[:, 0:1], fs2[:], AX.X, ALU.max, apply_absolute_value=True)
                ve.e.drain()
                ve.e.tensor_scalar(tiny[:, 1:2], a1c[:, jt:jt + 1], 1.0 / 127.0, None, ALU.mult)
                ve.e.drain()
                ve.e.tensor_tensor(tiny[:, 2:3], tiny[:, 1:2], wsc_bc[:, 5:6], ALU.mult)
                ve.e.drain()
                ve.e.tensor_scalar(tiny[:, 3:4], tiny[:, 2:3], C8192, None, ALU.mult)
                ve.e.drain()
                ve.e.tensor_tensor(tiny[:, 4:5], tiny[:, 3:4], tiny[:, 0:1], ALU.mult)
                ve.e.drain()
                ve.e.tensor_scalar(tiny[:, 5:6], tiny[:, 4:5], 1e-5, None, ALU.max)
                ve.e.drain()
                ve.e.reciprocal(tiny[:, 6:7], tiny[:, 5:6])
                ve.e.drain()
                ve.e.tensor_tensor(tiny[:, 7:8], tiny[:, 3:4], tiny[:, 6:7], ALU.mult)
                ve.e.drain()
                ve.e.tensor_scalar(tiny[:, 8:9], tiny[:, 7:8], 127.0, None, ALU.mult)
                ve.e.drain()
                ve.e.tensor_scalar(tiny[:, 9:10], tiny[:, 5:6], 1.0 / 127.0, None, ALU.mult)
                ve.e.drain()
                ve.inc(ve.e.tensor_tensor(dsc[:, jt:jt + 1], tiny[:, 9:10], wsc_bc[:, 6:7], ALU.mult))
                ve.e.drain()

            @block.vector
            def _(e, jj=jj):
                ve.e.drain()
                ve.e.tensor_scalar(ms2[:], fs2[:], tiny[:, 8:9], None, ALU.mult)
                ve.e.drain()
                ve.e.tensor_copy(q2i[:], ms2[:])
                ve.e.drain()
                ve.inc(ve.e.tensor_scalar(q2b[:], q2i[:], 127.0, -128.0, ALU.min, ALU.max))

            @block.gpsimd
            def _(e, jj=jj):
                gp.wait(ve)
                gp.dma(q2_tm[128 * jj:128 * (jj + 1), :], q2b[:])

    with ExitStack() as _sy3:
        sbd = lambda *a: _sy3.enter_context(nc.sbuf_tensor(*a))
        q2T = sbd(f"q2T_{ch}", [128, 16384], BF16)
        wdt = [sbd(f"wdt{ch}_{i}", [128, 512], BF16) for i in range(2)]
        dvs = sbd(f"dvs_{ch}", [128, 512], F32)
        pd = [_sy3.enter_context(nc.psum_tensor(f"pd{ch}_{i}", [128, 512], F32)) for i in range(4)]

        @block.scalar
        def _(e, ch=ch):
            sc.wait_dma(gp)
            for A in range(64):
                sc.dmaT(q2T[:, 256 * A:256 * (A + 1)], q2_tm[:, 128 * A:128 * (A + 1)])

        for og in range(4):
            for A in range(64):
                slot = A % 2
                slab, irow = A // 8, 128 * (A % 8)
                ldw = [0]

                @block.sync
                def _(e, og=og, A=A, slot=slot, slab=slab, irow=irow, ldw=ldw):
                    if og * 64 + A >= 2:
                        sy.wait(te)
                    elif og == 0 and A == 0:
                        sy.wait(te)
                        sy.wait(ve)
                    off = slab * (1024 * 2048) + irow * 2048 + 512 * og
                    ldw[0] = slot_dma(sy, wdS[slot], wdt[slot][:], bass.AP(wdT_all.tensor, off,
                                                                          [[2048, 128], [1, 512]]))

                @block.tensor
                def _(e, og=og, A=A, slot=slot, ldw=ldw):
                    te.e.wait_ge(wdS[slot], ldw[0])
                    if A == 0:
                        te.wait_dma(sc)
                        if og >= 2:
                            te.wait(ve)
                    bank = 2 * (og % 2)
                    for jj in range(2):
                        mm = te.e.matmul(pd[bank + jj][:],
                                         q2T[:, 256 * A + 128 * jj:256 * A + 128 * (jj + 1)],
                                         wdt[slot][:], start=(A == 0), stop=(A == 63))
                        if jj == 1:
                            te.inc(mm)

            for jj in range(2):
                jt = 2 * ch + jj

                @block.vector
                def _(e, og=og, jj=jj, jt=jt):
                    ve.wait(te)
                    ve.wait_dma(gp)
                    bank = 2 * (og % 2)
                    ve.inc(ve.e.tensor_scalar(dvs[:], pd[bank + jj][:], dsc[:, jt:jt + 1], None, ALU.mult))
                    ve.e.drain()

                @block.gpsimd
                def _(e, og=og, jj=jj, jt=jt):
                    gp.wait(ve)
                    gp.dma(acc_d[128 * jt:128 * (jt + 1), 512 * og:512 * (og + 1)], dvs[:])


def attn_phase(nc, block, sy, gp, sc, ve, te, ccS, cc_n,
               killbt, dmt, ones_b,
               q_tm, kT_own, v_own_d, kT_all, v_all, wo_in, linv_d, acc_d,
               slot_dma, woS, kvS):
    with ExitStack() as _sa:
        sba = lambda *a: _sa.enter_context(nc.sbuf_tensor(*a))
        kt2 = [sba(f"kt2_{i}", [128, 8192], BF16) for i in range(2)]
        vt2 = [sba(f"vt2_{i}", [128, 8192], BF16) for i in range(2)]
        qTt = sba("qTt", [128, 8192], BF16)
        PT = sba("PT", [128, 2048], BF16)
        opT = sba("opT", [128, 8192], F32)
        opn = sba("opn", [128, 8192], BF16)
        lacc = sba("lacc", [1, 8192], F32)
        lbc = sba("lbc", [128, 512], F32)
        wot = [sba(f"wot{i}", [128, 512], BF16) for i in range(2)]
        wsc2 = sba("wsc2", [128, 512], F32)
        pstE = _sa.enter_context(nc.psum_tensor("pstE", [128, 2048], F32))
        pstO = _sa.enter_context(nc.psum_tensor("pstO", [128, 2048], F32))

        for sect, w in ((block.sync, sy), (block.gpsimd, gp), (block.scalar, sc),
                        (block.vector, ve), (block.tensor, te)):
            def _f(e, w=w):
                w.e.wait_ge(ccS, cc_n[0])
            sect(_f)

        for qt in range(2):
            row0 = 512 if qt == 0 else 0
            niter = 16 if qt == 0 else 8
            kbase = 0 if qt == 0 else 16

            @block.scalar
            def _(e, qt=qt, row0=row0):
                sc.wait(ve)
                sc.wait(te)
                for h in range(NH):
                    sc.dmaT(qTt[:, 512 * h:512 * (h + 1)],
                            q_tm[row0:row0 + 512, 128 * h:128 * (h + 1)])

            add_ck = []
            kvte_ck = []
            for it in range(niter):
                buf = it % 2
                ldkv = [0]

                @block.gpsimd
                def _(e, qt=qt, it=it, buf=buf, row0=row0, ldkv=ldkv):
                    if it >= 2:
                        gp.wait(te, kvte_ck[it - 2])
                    if it == 0:
                        slot_dma(gp, kvS[buf], kt2[buf][:], bass.AP(kT_own.tensor, row0,
                                                                   [[1024, 128], [131072, 16], [1, 512]]))
                        ldkv[0] = slot_dma(gp, kvS[buf], vt2[buf][:], bass.AP(v_own_d.tensor, row0 * 2048,
                                                                             [[2048, 128], [262144, 4], [1, 2048]]))
                    else:
                        b = it - 1
                        slab = b if b <= 7 else 15 - b
                        half = 0 if b <= 7 else 1
                        koff = slab * (2048 * 1024) + half * 512
                        voff = slab * (1024 * 2048) + half * 512 * 2048
                        slot_dma(gp, kvS[buf], kt2[buf][:], bass.AP(kT_all.tensor, koff,
                                                                   [[1024, 128], [131072, 16], [1, 512]]))
                        ldkv[0] = slot_dma(gp, kvS[buf], vt2[buf][:], bass.AP(v_all.tensor, voff,
                                                                             [[2048, 128], [262144, 4], [1, 2048]]))

                for h in range(NH):
                    pst = pstE if h % 2 == 0 else pstO

                    @block.tensor
                    def _(e, qt=qt, it=it, h=h, buf=buf, pst=pst, ldkv=ldkv, si=len(add_ck)):
                        if h == 0:
                            te.e.wait_ge(kvS[buf], ldkv[0])
                            te.wait_dma(sc)
                        if si >= 2:
                            te.wait(ve, add_ck[si - 2])
                        elif si == 0:
                            te.wait(ve)
                        for s_ in range(4):
                            mm = te.e.matmul(pst[:, 512 * s_:512 * (s_ + 1)],
                                             kt2[buf][:, 512 * h + 128 * s_:512 * h + 128 * (s_ + 1)],
                                             qTt[:, 512 * h:512 * (h + 1)], start=True, stop=True)
                            if s_ == 3:
                                te.inc(mm)

                    if it == 0:
                        @block.vector
                        def _(e, h=h, pst=pst):
                            ve.wait(te)
                            ve.inc(ve.e.tensor_tensor(pst[:], pst[:], dmt[:], ALU.add))
                            ve.e.drain()

                    @block.scalar
                    def _(e, qt=qt, it=it, h=h, pst=pst, kbase=kbase):
                        sc.wait(te)
                        if it == 0:
                            sc.wait(ve)
                        sc.inc(sc.e.activation(PT[:], pst[:], AF.Exp,
                                               bias=killbt[:, kbase + it:kbase + it + 1], scale=1.0))

                    @block.tensor
                    def _(e, h=h, it=it, buf=buf, pst=pst):
                        te.wait(sc)
                        for s_ in range(4):
                            te.e.matmul(pst[:, 0:512],
                                        vt2[buf][:, 2048 * s_ + 128 * h:2048 * s_ + 128 * (h + 1)],
                                        PT[:, 512 * s_:512 * (s_ + 1)], start=(s_ == 0), stop=(s_ == 3))
                        for s_ in range(4):
                            mm = te.e.matmul(pst[0:1, 512:1024], ones_b[:], PT[:, 512 * s_:512 * (s_ + 1)],
                                             start=(s_ == 0), stop=(s_ == 3))
                            if s_ == 3:
                                te.inc(mm)

                    @block.vector
                    def _(e, h=h, it=it, pst=pst):
                        ve.wait(te)
                        if it == 0:
                            ve.e.tensor_copy(opT[:, 512 * h:512 * (h + 1)], pst[:, 0:512])
                            ve.e.drain()
                            ve.inc(ve.e.tensor_copy(lacc[:, 512 * h:512 * (h + 1)], pst[0:1, 512:1024]))
                        else:
                            ve.e.tensor_tensor(opT[:, 512 * h:512 * (h + 1)],
                                               opT[:, 512 * h:512 * (h + 1)], pst[:, 0:512], ALU.add)
                            ve.e.drain()
                            ve.inc(ve.e.tensor_tensor(lacc[:, 512 * h:512 * (h + 1)],
                                                      lacc[:, 512 * h:512 * (h + 1)], pst[0:1, 512:1024], ALU.add))
                        ve.e.drain()
                    add_ck.append(ve.n)
                kvte_ck.append(te.n)

            @block.vector
            def _(e):
                ve.inc(ve.e.reciprocal(lacc[:], lacc[:]))
                ve.e.drain()

            @block.gpsimd
            def _(e):
                gp.wait(ve)
                gp.dma(linv_d[:], lacc[:])

            for h in range(NH):
                ldc = [0]

                @block.sync
                def _(e, h=h, ldc=ldc):
                    sy.wait_dma(gp)
                    sy.wait(ve)
                    ldc[0] = sy.dma(lbc[:], bass.AP(linv_d.tensor, 512 * h, [[0, 128], [1, 512]]))

                @block.vector
                def _(e, h=h, ldc=ldc):
                    ve.wait_dma(sy, ldc[0])
                    ve.inc(ve.e.tensor_tensor(opn[:, 512 * h:512 * (h + 1)],
                                       opT[:, 512 * h:512 * (h + 1)], lbc[:], ALU.mult))
                    ve.e.drain()

            if True:
                pw = [pstE[:, 512 * i:512 * (i + 1)] for i in range(4)]
                for eg in range(4):
                    for h in range(NH):
                        slot = h % 2
                        ldw = [0]

                        @block.gpsimd
                        def _(e, eg=eg, h=h, slot=slot, ldw=ldw):
                            if eg * NH + h >= 2:
                                gp.wait(te)
                            ldw[0] = slot_dma(gp, woS[slot], wot[slot][:],
                                              wo_in[128 * h:128 * (h + 1), 512 * eg:512 * (eg + 1)])

                        @block.tensor
                        def _(e, eg=eg, h=h, slot=slot, ldw=ldw):
                            te.e.wait_ge(woS[slot], ldw[0])
                            if h == 0:
                                te.wait(ve)
                            for qs_ in range(4):
                                mm = te.e.matmul(pw[qs_],
                                                 opn[:, 512 * h + 128 * qs_:512 * h + 128 * (qs_ + 1)],
                                                 wot[slot][:], start=(h == 0), stop=(h == NH - 1))
                                if qs_ == 3:
                                    te.inc(mm)

                    for qs_ in range(4):
                        @block.vector
                        def _(e, eg=eg, qs_=qs_):
                            ve.wait(te)
                            ve.wait_dma(gp)
                            ve.inc(ve.e.tensor_copy(wsc2[:], pw[qs_]))
                            ve.e.drain()

                        @block.gpsimd
                        def _(e, eg=eg, qs_=qs_, qt=qt):
                            gp.wait(ve)
                            row = (512 if qt == 0 else 0) + 128 * qs_
                            gp.dma(acc_d[row:row + 128, 512 * eg:512 * (eg + 1)],
                                   wsc2[:], accum_op=ALU.add)


if __name__ == "__main__":
    pass
